# version 01 (final)
# baseline (speedup 1.0000x reference)
import os
import numpy as np
import ml_dtypes
import concourse.bass as bass
import concourse.mybir as mybir
from concourse.bass_utils import run_bass_kernel_spmd
from contextlib import ExitStack

F32 = mybir.dt.float32
BF16 = mybir.dt.bfloat16
AF = mybir.ActivationFunctionType
ALU = mybir.AluOpType
AX = mybir.AxisListType

T = 2048
D = 1024
NT = T // 128
NC_ = D // 128
H = 16
HD = 64
DL = 512
NR = 64
E8 = 8
I0 = 2048
F1 = 2048
EPS_MLA = 1.1920929e-07
EPS_LN = 1e-5
EPS_RMS = 1e-5
NEWTON_ITERS = int(os.environ.get('KMK_NEWTON', '6'))

DEBUG_PHASE = int(os.environ.get("KMK_DEBUG_PHASE", "0"))
N_HEADS = int(os.environ.get("KMK_HEADS", str(H)))
MLA_PART = int(os.environ.get("KMK_MLA_PART", "3"))

ENGS = ("sync", "tensor", "vector", "scalar", "gpsimd")
OWN = {"vector": "dve", "scalar": "act", "gpsimd": "gp"}


class Bld:

    def __init__(self, nc, es):
        self.nc = nc
        self.q = {e: [] for e in ENGS}
        self.sems = {}
        self.es = es
        for name in ("wdma", "xdma", "odma", "pe", "dve", "act", "gp", "bar"):
            self.sems[name] = [es.enter_context(nc.semaphore(name)), 0]
        for i in range(6):
            self.sems[f"dp{i}"] = [es.enter_context(nc.semaphore(f"dp{i}")), 0]
        for i in range(4):
            self.sems[f"gq{i}"] = [es.enter_context(nc.semaphore(f"gq{i}")), 0]
        self._dma_rot = 0
        self._gq_rot = 0

    def C(self, name):
        return self.sems[name][1]

    def wait(self, eng, name, val):
        h = self.sems[name][0]
        self.q[eng].append(lambda E, h=h, v=val: E.wait_ge(h, v))

    def emit(self, eng, fn, sem=None, k=1, selfwait=False):
        own = OWN.get(eng)
        if sem is None:
            sem = own
        if sem == "":
            sem = None
        if selfwait and own is not None:
            self.wait(eng, own, self.sems[own][1])
        if sem is not None:
            s = self.sems[sem]
            s[1] += k
            h = s[0]
            def go(E, fn=fn, h=h, k=k):
                fn(E).then_inc(h, k)
            self.q[eng].append(go)
            return s[1]
        self.q[eng].append(lambda E, fn=fn: fn(E))
        return None

    def dma(self, eng, fn):
        if eng == "gpsimd":
            nm = f"gq{self._gq_rot % 4}"
            self._gq_rot += 1
        else:
            nm = f"dp{self._dma_rot % 6}"
            self._dma_rot += 1
        s = self.sems[nm]
        if s[1] > 0:
            self.wait(eng, nm, s[1])
        s[1] += 16
        h, v = s[0], s[1]
        self.q[eng].append(lambda E, fn=fn, h=h: fn(E).then_inc(h, 16))
        return (nm, v)

    def barrier(self, engs=ENGS):
        names = ["pe", "dve", "act", "gp", "wdma", "xdma", "odma"]
        names += [f"dp{i}" for i in range(6)] + [f"gq{i}" for i in range(4)]
        snap = [(nm, self.sems[nm][1]) for nm in names]
        for e in engs:
            for nm, v in snap:
                if v > 0:
                    self.wait(e, nm, v)

    def replay(self, block):
        b = self

        @block.sync
        def _(E):
            for f in b.q["sync"]:
                f(E)

        @block.tensor
        def _(E):
            for f in b.q["tensor"]:
                f(E)

        @block.vector
        def _(E):
            for f in b.q["vector"]:
                f(E)

        @block.scalar
        def _(E):
            for f in b.q["scalar"]:
                f(E)

        @block.gpsimd
        def _(E):
            for f in b.q["gpsimd"]:
                f(E)


class Pipe:

    def __init__(self, b, psums):
        self.b = b
        self.ps = psums
        self.last_ev = {}
        self.j = 0

    def job(self, mms, evac, evac_eng="scalar", pre=(), evac_pre=()):
        b = self.b
        par = self.j % len(self.ps)
        ps = self.ps[par]
        self.j += 1
        if par in self.last_ev:
            nm, cnt = self.last_ev[par]
            b.wait("tensor", nm, cnt)
        for nm, cnt in pre:
            b.wait("tensor", nm, cnt)
        for i, mmfn in enumerate(mms):
            b.emit("tensor", (lambda E, f=mmfn, ps=ps: f(E, ps)),
                   sem="pe" if i == len(mms) - 1 else "")
        pm = b.C("pe")
        if evac is None:
            return pm
        eng = evac_eng
        b.wait(eng, "pe", pm)
        for nm, cnt in evac_pre:
            b.wait(eng, nm, cnt)
        b.emit(eng, lambda E, f=evac, ps=ps: f(E, ps))
        self.last_ev[par] = (OWN[eng], b.C(OWN[eng]))
        return pm


class MK:
    def __init__(self, debug_phase=0):
        self.nc = bass.Bass("TRN2", target_bir_lowering=False)
        self.es = ExitStack()
        self.debug_phase = debug_phase
        self.dbg = {}

    def sb(self, name, shape, dt):
        return self.es.enter_context(self.nc.sbuf_tensor(name, shape, dt))

    def io(self):
        nc = self.nc

        def din(name, shape):
            return nc.dram_tensor(name, shape, F32, kind="ExternalInput").ap()

        self.x_in = din("x", [T, D])
        w = {}
        for name, shape in [
            ("l0a_wdq", [DL, HD]), ("l0a_wuq", [HD, DL]), ("l0a_wdkv", [2 * DL, HD]),
            ("l0a_wukv", [2 * HD, 2 * DL]), ("l0a_wqr", [NR, HD]), ("l0a_wkr", [NR, HD]),
            ("l0a_wo", [HD, HD]), ("l0a_qnw", [1, HD]), ("l0a_knw", [1, HD]),
            ("l0f_rw", [E8, D]), ("l0f_rb", [1, E8]),
            ("l0f_gup", [E8, 2 * I0, D]), ("l0f_gupb", [E8, 2 * I0]),
            ("l0f_dp", [E8, D, I0]), ("l0f_dpb", [E8, D]),
            ("l1a_wq", [D, D]), ("l1a_wk", [D, D]), ("l1a_wv", [D, D]), ("l1a_wo", [D, D]),
            ("l1f_ln1", [1, D]), ("l1f_ln2", [1, D]),
            ("l1f_wq", [D, D]), ("l1f_wk", [D, 256]), ("l1f_wv", [D, 256]), ("l1f_wo", [D, D]),
            ("l1f_wr", [D, E8]),
            ("l1f_wg", [E8, D, F1]), ("l1f_wu", [E8, D, F1]), ("l1f_wd", [E8, F1, D]),
            ("rope_cos", [T, 32]), ("rope_sin", [T, 32]),
        ]:
            w[name] = din(name, shape)
        self.w = w
        self.out = nc.dram_tensor("out", [T, D], F32, kind="ExternalOutput").ap()
        if self.debug_phase:
            for name, shape in [("dbg_a", [128, NT * D]), ("dbg_b", [128, 64])]:
                self.dbg[name] = nc.dram_tensor(name, shape, F32, kind="ExternalOutput").ap()

        self.b = Bld(nc, self.es)
        self.x_acc = self.sb("x_acc", [128, NT * D], F32)
        self.xnf = self.sb("xnf", [128, NC_ * T], BF16)
        self.identb = self.sb("identb", [128, 128], BF16)
        self.identf = self.sb("identf", [128, 128], F32)
        self.ones_bf = self.sb("ones_bf", [128, 128], BF16)
        self.stats = self.sb("stats", [128, 64], F32)
        self.sjunk = self.sb("sjunk", [128, 1024], BF16)
        self.consts = self.sb("consts", [128, 8], F32)
        self.xn_t = self.sb("xn_t", [128, 1024], BF16)
        self.p_tp = self.es.enter_context(self.nc.psum_tensor("p_tp", [128, 1024], BF16))

    def consts_phase(self):
        b = self.b
        b.emit("gpsimd", lambda E: E.memset(self.identb[:], 0.0))
        b.emit("gpsimd", lambda E: E.affine_select(
            out=self.identb[:], in_=self.identb[:], compare_op=ALU.not_equal, fill=1.0,
            base=0, pattern=[[-1, 128]], channel_multiplier=1), selfwait=True)
        b.emit("gpsimd", lambda E: E.memset(self.identf[:], 0.0))
        b.emit("gpsimd", lambda E: E.affine_select(
            out=self.identf[:], in_=self.identf[:], compare_op=ALU.not_equal, fill=1.0,
            base=0, pattern=[[-1, 128]], channel_multiplier=1), selfwait=True)
        b.emit("gpsimd", lambda E: E.memset(self.ones_bf[:], 1.0))
        b.emit("gpsimd", lambda E: E.memset(self.consts[:, 0:1], EPS_LN))
        b.emit("gpsimd", lambda E: E.memset(self.consts[:, 1:2], EPS_MLA), sem="gp")
        self.consts_done = b.C("gp")

    def ln_to_xnf(self, src):
        b, stats, sjunk = self.b, self.stats, self.sjunk
        s_sum = stats[:, 0:16]
        s_sq = stats[:, 16:32]
        mean = stats[:, 32:48]
        var = stats[:, 48:64]
        for tt in range(NT):
            xt = src[:, tt * D:(tt + 1) * D]
            b.emit("vector", lambda E, xt=xt, tt=tt: E.tensor_scalar(
                out=sjunk[:, 0:1024], in0=xt, scalar1=1.0, scalar2=None,
                op0=ALU.mult, op1=ALU.add, accum_out=s_sum[:, tt:tt + 1]), selfwait=True)
            b.emit("vector", lambda E, xt=xt, tt=tt: E.scalar_tensor_tensor(
                out=sjunk[:, 0:1024], in0=xt, scalar=1.0, in1=xt,
                op0=ALU.mult, op1=ALU.mult, accum_out=s_sq[:, tt:tt + 1]), selfwait=True)
        b.emit("vector", lambda E: E.tensor_scalar(
            out=mean, in0=s_sum, scalar1=1.0 / D, scalar2=None, op0=ALU.mult), selfwait=True)
        b.emit("vector", lambda E: E.scalar_tensor_tensor(
            out=var, in0=mean, scalar=1.0, in1=mean, op0=ALU.mult, op1=ALU.mult), selfwait=True)
        b.emit("vector", lambda E: E.scalar_tensor_tensor(
            out=var, in0=s_sq, scalar=1.0 / D, in1=var, op0=ALU.mult, op1=ALU.subtract),
            selfwait=True)
        v_stats = b.C("dve")
        b.wait("scalar", "dve", v_stats)
        b.wait("scalar", "gp", self.consts_done)
        rstd = s_sum
        b.emit("scalar", lambda E: E.activation(out=var, in_=var, func=AF.Ln,
                                                bias=self.consts[:, 0:1], scale=1.0))
        b.emit("scalar", lambda E: E.activation(out=rstd, in_=var, func=AF.Exp,
                                                bias=0.0, scale=-0.5), selfwait=True)
        a_rstd = b.C("act")
        b.wait("vector", "act", a_rstd)
        negmr = s_sq
        b.emit("vector", lambda E: E.scalar_tensor_tensor(
            out=negmr, in0=mean, scalar=-1.0, in1=rstd, op0=ALU.mult, op1=ALU.mult),
            selfwait=True)
        b.wait("tensor", "gp", self.consts_done)
        pe_after = {}
        evac_done = {}
        for tt in range(NT):
            xt = src[:, tt * D:(tt + 1) * D]
            if tt > 0:
                b.wait("vector", "pe", pe_after[tt - 1])
            b.emit("vector", lambda E, xt=xt, tt=tt: E.tensor_scalar(
                out=self.xn_t[:], in0=xt, scalar1=rstd[:, tt:tt + 1],
                scalar2=negmr[:, tt:tt + 1], op0=ALU.mult, op1=ALU.add), selfwait=(tt == 0))
            v_tile = b.C("dve")
            b.wait("tensor", "dve", v_tile)
            if tt > 0:
                b.wait("tensor", "act", evac_done[tt - 1])
            for c in range(NC_):
                b.emit("tensor", lambda E, c=c: E.transpose(
                    out=self.p_tp[:, c * 128:(c + 1) * 128],
                    in_=self.xn_t[:, c * 128:(c + 1) * 128],
                    identity=self.identb[:]), sem="pe" if c == NC_ - 1 else "")
            pe_after[tt] = b.C("pe")
            b.wait("scalar", "pe", pe_after[tt])
            for c in range(NC_):
                b.emit("scalar", lambda E, c=c, tt=tt: E.copy(
                    out=self.xnf[:, c * T + tt * 128: c * T + (tt + 1) * 128],
                    in_=self.p_tp[:, c * 128:(c + 1) * 128]))
            evac_done[tt] = b.C("act")
        return rstd, negmr

    def phase0(self):
        b = self.b
        for tt in range(NT):
            b.emit("gpsimd", lambda E, tt=tt: E.dma_start(
                out=self.x_acc[:, tt * D:(tt + 1) * D],
                in_=self.x_in[tt * 128:(tt + 1) * 128, :]), sem="xdma", k=16)
        x_loaded = b.C("xdma")
        self.consts_phase()
        b.wait("vector", "xdma", x_loaded)
        self.ln_to_xnf(self.x_acc)
        b.barrier()

    def mla_prep(self):
        if os.environ.get("KMK_SKIP_PREP"):
            return
        b, nc, w = self.b, self.nc, self.w
        es_mw = ExitStack()
        self._es_mw = es_mw
        self.wdqt = es_mw.enter_context(nc.sbuf_tensor("wdqt", [128, 512], BF16))
        self.wuqt = es_mw.enter_context(nc.sbuf_tensor("wuqt", [128, 256], BF16))
        self.wdkvt = es_mw.enter_context(nc.sbuf_tensor("wdkvt", [128, 1024], BF16))
        self.wukv_kt = es_mw.enter_context(nc.sbuf_tensor("wukv_kt", [128, 512], BF16))
        self.wukv_vt = es_mw.enter_context(nc.sbuf_tensor("wukv_vt", [128, 512], BF16))
        self.wqrt = es_mw.enter_context(nc.sbuf_tensor("wqrt", [128, 64], BF16))
        self.wkrt = es_mw.enter_context(nc.sbuf_tensor("wkrt", [128, 64], BF16))
        self.wot = es_mw.enter_context(nc.sbuf_tensor("wot", [128, 64], BF16))
        self.wqcol = es_mw.enter_context(nc.sbuf_tensor("wqcol", [128, 2], F32))
        self.svec = es_mw.enter_context(nc.sbuf_tensor("svec", [128, 1], F32))
        self.wot2 = es_mw.enter_context(nc.sbuf_tensor("wot2", [128, 128], BF16))
        es_prep = ExitStack()
        self.stage = es_prep.enter_context(nc.sbuf_tensor("stage", [128, 1024], F32))
        self.stageb = es_prep.enter_context(nc.sbuf_tensor("stageb", [128, 1024], BF16))
        self._es_prep = es_prep
        p_w = self.p_tp

        def prep_T(dst, dram_ap, nrow, ncol, dup):
            nblk = (nrow + 127) // 128
            lds = []
            for bk in range(nblk):
                r0, r1 = bk * 128, min(nrow, bk * 128 + 128)
                lds.append(b.dma("sync", lambda E, bk=bk, r0=r0, r1=r1, ap=dram_ap: E.dma_start(
                    out=self.stage[0:r1 - r0, bk * ncol:(bk + 1) * ncol],
                    in_=ap[r0:r1, :])))
            for nm, v in lds:
                b.wait("vector", nm, v)
            b.emit("vector", lambda E, n=nblk * ncol: E.tensor_copy(
                out=self.stageb[:, 0:n], in_=self.stage[:, 0:n]), selfwait=True)
            cv = b.C("dve")
            b.wait("tensor", "dve", cv)
            for bk in range(nblk):
                rows = min(128, nrow - bk * 128)
                b.emit("tensor", lambda E, bk=bk, rows=rows: E.transpose(
                    out=p_w[0:ncol, bk * 128:bk * 128 + rows],
                    in_=self.stageb[0:rows, bk * ncol:(bk + 1) * ncol],
                    identity=self.identb[0:rows, 0:rows]), sem="pe" if bk == nblk - 1 else "")
            pt = b.C("pe")
            b.wait("scalar", "pe", pt)
            b.emit("scalar", lambda E, dst=dst, nrow=nrow, ncol=ncol: E.copy(
                out=dst[0:ncol, 0:nrow], in_=p_w[0:ncol, 0:nrow]))
            ev = b.C("act")
            if dup:
                b.wait("vector", "act", ev)
                b.emit("vector", lambda E, dst=dst, nrow=nrow: E.tensor_copy(
                    out=dst[64:128, 0:nrow], in_=dst[0:64, 0:nrow]), selfwait=True)
            b.wait("sync", "dve", cv)
            b.wait("vector", "pe", pt)
            b.wait("tensor", "act", ev)

        b.wait("tensor", "gp", self.consts_done)
        PREP = int(os.environ.get("KMK_PREP", "255"))
        if PREP & 1:
            prep_T(self.wdqt, w["l0a_wdq"], DL, HD, dup=True)
        if PREP & 2:
            prep_T(self.wdkvt, w["l0a_wdkv"], 2 * DL, HD, dup=True)
        if PREP & 4:
            prep_T(self.wqrt, w["l0a_wqr"], NR, HD, dup=False)
            prep_T(self.wkrt, w["l0a_wkr"], NR, HD, dup=True)
            prep_T(self.wot, w["l0a_wo"], HD, HD, dup=True)

        if not (PREP & 8):
            if PREP & 128:
                self._es_prep.close()
            b.barrier()
            return
        nmv = b.dma("sync", lambda E: E.dma_start(out=self.stage[0:64, 0:512], in_=w["l0a_wuq"]))
        b.wait("vector", nmv[0], nmv[1])
        b.emit("vector", lambda E: E.tensor_copy(out=self.stageb[0:64, 0:512],
                                                 in_=self.stage[0:64, 0:512]), selfwait=True)
        cv = b.C("dve")
        b.wait("tensor", "dve", cv)
        for lc in range(4):
            b.emit("tensor", lambda E, lc=lc: E.transpose(
                out=p_w[:, lc * 64:(lc + 1) * 64],
                in_=self.stageb[0:64, lc * 128:(lc + 1) * 128],
                identity=self.identb[0:64, 0:64]), sem="pe" if lc == 3 else "")
        pt = b.C("pe")
        b.wait("scalar", "pe", pt)
        b.emit("scalar", lambda E: E.copy(out=self.wuqt[:, 0:256], in_=p_w[:, 0:256]))
        ev = b.C("act")
        b.wait("tensor", "act", ev)
        b.wait("sync", "dve", cv)
        b.wait("vector", "pe", pt)

        if not (PREP & 16):
            self._es_prep.close()
            b.barrier()
            return
        nmv = b.dma("sync", lambda E: E.dma_start(out=self.stage[:, 0:1024], in_=w["l0a_wukv"]))
        b.wait("vector", nmv[0], nmv[1])
        b.emit("vector", lambda E: E.tensor_copy(out=self.stageb[:, 0:1024],
                                                 in_=self.stage[:, 0:1024]), selfwait=True)
        cv = b.C("dve")
        b.wait("tensor", "dve", cv)
        for lc in range(8):
            b.emit("tensor", lambda E, lc=lc: E.transpose(
                out=p_w[:, lc * 64:(lc + 1) * 64],
                in_=self.stageb[0:64, lc * 128:(lc + 1) * 128],
                identity=self.identb[0:64, 0:64]), sem="")
        b.emit("vector", lambda E: E.tensor_copy(
            out=self.xn_t[0:64, 0:1024], in_=self.stageb[64:128, 0:1024]), selfwait=True)
        cv2 = b.C("dve")
        b.wait("tensor", "dve", cv2)
        for lc in range(8):
            b.emit("tensor", lambda E, lc=lc: E.transpose(
                out=p_w[:, 512 + lc * 64:512 + (lc + 1) * 64],
                in_=self.xn_t[0:64, lc * 128:(lc + 1) * 128],
                identity=self.identb[0:64, 0:64]), sem="pe" if lc == 7 else "")
        pt = b.C("pe")
        b.wait("scalar", "pe", pt)
        b.emit("scalar", lambda E: E.copy(out=self.wukv_kt[:, 0:512], in_=p_w[:, 0:512]))
        b.emit("scalar", lambda E: E.copy(out=self.wukv_vt[:, 0:512], in_=p_w[:, 512:1024]))
        ev = b.C("act")
        b.wait("tensor", "act", ev)
        b.wait("sync", "dve", cv)
        b.wait("vector", "pe", pt)

        if not (PREP & 32):
            self._es_prep.close()
            b.barrier()
            return
        with nc.psum_tensor("p_wf", [128, 128], F32) as p_wf:
            b.wait("gpsimd", "dve", b.C("dve"))
            b.emit("gpsimd", lambda E: E.memset(self.stage[:, 0:128], 0.0), sem="gp")
            gmz = b.C("gp")
            b.wait("sync", "gp", gmz)
            n1 = b.dma("sync", lambda E: E.dma_start(out=self.stage[0:1, 0:64], in_=w["l0a_qnw"]))
            n2 = b.dma("sync", lambda E: E.dma_start(out=self.stage[0:1, 64:128], in_=w["l0a_knw"]))
            b.wait("tensor", n1[0], n1[1])
            b.wait("tensor", n2[0], n2[1])
            b.emit("tensor", lambda E: E.transpose(
                out=p_wf[:, 0:128], in_=self.stage[0:128, 0:128],
                identity=self.identf[:]), sem="pe")
            pt = b.C("pe")
            b.wait("scalar", "pe", pt)
            b.emit("scalar", lambda E: E.copy(out=self.wqcol[:, 0:1], in_=p_wf[:, 0:1]))
            ev = b.C("act")
            b.wait("vector", "act", ev)
            b.emit("vector", lambda E: E.tensor_copy(
                out=self.wqcol[0:64, 1:2], in_=self.wqcol[64:128, 0:1]), selfwait=True)
            b.wait("gpsimd", "act", ev)
            b.emit("gpsimd", lambda E: E.memset(self.svec[:], 0.125))
            gm = b.C("gp")
            b.wait("vector", "gp", gm)
            b.emit("vector", lambda E: E.scalar_tensor_tensor(
                out=self.svec[0:64, 0:1], in0=self.wqcol[0:64, 0:1], scalar=0.125,
                in1=self.wqcol[0:64, 1:2], op0=ALU.mult, op1=ALU.mult), selfwait=True)
            b.emit("vector", lambda E: E.tensor_scalar(
                out=self.wqrt[0:64, 0:64], in0=self.wqrt[0:64, 0:64],
                scalar1=self.wqcol[0:64, 0:1], scalar2=None, op0=ALU.mult), selfwait=True)
            b.wait("tensor", "dve", b.C("dve"))
            b.emit("gpsimd", lambda E: E.memset(self.wot2[:], 0.0), sem="gp")
            gz = b.C("gp")
            b.wait("vector", "gp", gz)
            b.emit("vector", lambda E: E.tensor_copy(
                out=self.wot2[0:64, 0:64], in_=self.wot[0:64, 0:64]), selfwait=True)
            b.emit("vector", lambda E: E.tensor_copy(
                out=self.wot2[64:128, 64:128], in_=self.wot[64:128, 0:64]), selfwait=True)
            b.barrier()
        self._es_prep.close()

    def mla(self):
        if os.environ.get("KMK_SKIP_MLA"):
            self.tap = {}
            return
        b, nc = self.b, self.nc
        es2 = ExitStack()
        cq_bf = es2.enter_context(nc.sbuf_tensor("cq_bf", [128, 4 * T], BF16))
        ckv_bf = cq_bf
        q_full = es2.enter_context(nc.sbuf_tensor("q_full", [128, T], BF16))
        k_full = es2.enter_context(nc.sbuf_tensor("k_full", [128, T], BF16))
        v_t = es2.enter_context(nc.sbuf_tensor("v_t", [128, NT * HD], BF16))
        qc_f32 = es2.enter_context(nc.sbuf_tensor("qc_f32", [64, T], F32))
        kc_f32 = qc_f32
        sq_bf = es2.enter_context(nc.sbuf_tensor("sq_bf", [64, T], BF16))
        msrow = es2.enter_context(nc.sbuf_tensor("msrow", [1, T], F32))
        lnrow = msrow
        rstd_row = es2.enter_context(nc.sbuf_tensor("rstd_row", [1, T], BF16))
        rstd_bc = es2.enter_context(nc.sbuf_tensor("rstd_bc", [64, T], BF16))
        z_bf = es2.enter_context(nc.sbuf_tensor("z_bf", [128, T], BF16))
        p_bf = es2.enter_context(nc.sbuf_tensor("p_bf", [128, T], BF16))
        pT_sb = es2.enter_context(nc.sbuf_tensor("pT_sb", [128, 1024], BF16))
        ctx_all = es2.enter_context(nc.sbuf_tensor("ctx_all", [128, NC_ * T], BF16))
        cmask = es2.enter_context(nc.sbuf_tensor("cmask", [128, 128], BF16))
        acol = es2.enter_context(nc.sbuf_tensor("acol", [128, 8], F32))

        b.emit("gpsimd", lambda E: E.memset(cmask[:], 0.0))
        b.emit("gpsimd", lambda E: E.affine_select(
            out=cmask[:], in_=cmask[:], compare_op=ALU.is_ge, fill=-1e9,
            base=0, pattern=[[-1, 128]], channel_multiplier=1), selfwait=True)
        gp_cmask = b.C("gp")
        b.wait("vector", "gp", gp_cmask)

        pes = ExitStack()
        pe1 = pes.enter_context(nc.psum_tensor("pe1", [128, 512], F32))
        pe2 = pes.enter_context(nc.psum_tensor("pe2", [128, 512], F32))
        p_ms = pes.enter_context(nc.psum_tensor("p_ms", [1, 512], F32))
        p_ctx = pes.enter_context(nc.psum_tensor("p_ctx", [64, 128], F32))

        for h in range(N_HEADS):
            chunk, poff = h // 2, 64 * (h % 2)
            xh = self.xnf[poff:poff + 64, chunk * T:(chunk + 1) * T]
            P = Pipe(b, [pe1, pe2])

            for lc in range(4):
                for tb in range(4):
                    P.job([lambda E, ps, lc=lc, tb=tb, xh=xh, poff=poff: E.matmul(
                        out=ps[:, 0:512],
                        lhsT=self.wdqt[poff:poff + 64, lc * 128:(lc + 1) * 128],
                        rhs=xh[:, tb * 512:(tb + 1) * 512], start=True, stop=True)],
                        lambda E, ps, lc=lc, tb=tb: E.copy(
                            out=cq_bf[:, lc * T + tb * 512: lc * T + (tb + 1) * 512],
                            in_=ps[:, 0:512]))
            for tb in range(4):
                P.job([(lambda E, ps, lc=lc, tb=tb: E.matmul(
                    out=ps[0:64, 0:512], lhsT=self.wuqt[:, lc * 64:(lc + 1) * 64],
                    rhs=cq_bf[:, lc * T + tb * 512: lc * T + (tb + 1) * 512],
                    start=(lc == 0), stop=(lc == 3))) for lc in range(4)],
                    lambda E, ps, tb=tb: E.copy(
                        out=qc_f32[:, tb * 512:(tb + 1) * 512], in_=ps[0:64, 0:512]))
            a_qc = b.C("act")
            b.wait("vector", "act", a_qc)
            for tb in range(4):
                b.emit("vector", lambda E, tb=tb: E.scalar_tensor_tensor(
                    out=sq_bf[:, tb * 512:(tb + 1) * 512],
                    in0=qc_f32[:, tb * 512:(tb + 1) * 512], scalar=1.0,
                    in1=qc_f32[:, tb * 512:(tb + 1) * 512],
                    op0=ALU.mult, op1=ALU.mult), selfwait=(tb == 0))
            v_sq = b.C("dve")
            Pm = Pipe(b, [p_ms])
            for tb in range(4):
                Pm.job([lambda E, ps, tb=tb: E.matmul(
                    out=ps[0:1, 0:512], lhsT=self.ones_bf[0:64, 0:1],
                    rhs=sq_bf[:, tb * 512:(tb + 1) * 512], start=True, stop=True)],
                    lambda E, ps, tb=tb: E.copy(
                        out=msrow[0:1, tb * 512:(tb + 1) * 512], in_=ps[0:1, 0:512]),
                    pre=[("dve", v_sq)])
            b.emit("scalar", lambda E: E.activation(
                out=lnrow[0:1, :], in_=msrow[0:1, :], func=AF.Ln,
                bias=self.consts[0:1, 1:2], scale=1.0 / HD), selfwait=True)
            b.emit("scalar", lambda E: E.activation(
                out=rstd_row[0:1, :], in_=lnrow[0:1, :], func=AF.Exp,
                bias=0.0, scale=-0.5), selfwait=True)
            a_rs = b.C("act")
            for tb in range(4):
                P.job([lambda E, ps, tb=tb: E.matmul(
                    out=ps[0:64, 0:512], lhsT=self.ones_bf[0:1, 0:64],
                    rhs=rstd_row[0:1, tb * 512:(tb + 1) * 512], start=True, stop=True)],
                    lambda E, ps, tb=tb: E.copy(
                        out=rstd_bc[:, tb * 512:(tb + 1) * 512], in_=ps[0:64, 0:512]),
                    pre=[("act", a_rs)])
            a_bc = b.C("act")
            b.wait("vector", "act", a_bc)
            for tb in range(4):
                b.emit("vector", lambda E, tb=tb: E.tensor_mul(
                    out=q_full[0:64, tb * 512:(tb + 1) * 512],
                    in0=qc_f32[:, tb * 512:(tb + 1) * 512],
                    in1=rstd_bc[:, tb * 512:(tb + 1) * 512]), selfwait=(tb == 0))
            v_qcs = b.C("dve")
            for tb in range(4):
                P.job([lambda E, ps, tb=tb: E.matmul(
                    out=ps[0:64, 0:512], lhsT=self.wqrt[0:64, 0:64],
                    rhs=q_full[0:64, tb * 512:(tb + 1) * 512], start=True, stop=True)],
                    lambda E, ps, tb=tb: E.copy(
                        out=q_full[64:128, tb * 512:(tb + 1) * 512], in_=ps[0:64, 0:512]),
                    pre=[("dve", v_qcs)])
            a_qr = b.C("act")
            b.wait("vector", "act", a_qr)
            b.emit("vector", lambda E: E.tensor_scalar(
                out=q_full[:], in0=q_full[:], scalar1=self.svec[:, 0:1], scalar2=None,
                op0=ALU.mult), selfwait=True)
            v_qdone = b.C("dve")

            for th in range(2):
                t0 = th * 1024
                for lc in range(8):
                    for tb in range(2):
                        P.job([lambda E, ps, lc=lc, tb=tb, t0=t0, xh=xh, poff=poff: E.matmul(
                            out=ps[:, 0:512],
                            lhsT=self.wdkvt[poff:poff + 64, lc * 128:(lc + 1) * 128],
                            rhs=xh[:, t0 + tb * 512: t0 + (tb + 1) * 512],
                            start=True, stop=True)],
                            lambda E, ps, lc=lc, tb=tb: E.copy(
                                out=ckv_bf[:, lc * 1024 + tb * 512: lc * 1024 + (tb + 1) * 512],
                                in_=ps[:, 0:512]))
                for tb in range(2):
                    P.job([(lambda E, ps, lc=lc, tb=tb: E.matmul(
                        out=ps[0:64, 0:512], lhsT=self.wukv_kt[:, lc * 64:(lc + 1) * 64],
                        rhs=ckv_bf[:, lc * 1024 + tb * 512: lc * 1024 + (tb + 1) * 512],
                        start=(lc == 0), stop=(lc == 7))) for lc in range(8)],
                        lambda E, ps, tb=tb, t0=t0: E.copy(
                            out=kc_f32[:, t0 + tb * 512: t0 + (tb + 1) * 512],
                            in_=ps[0:64, 0:512]), evac_pre=[("dve", v_qcs)])
                for tl in range(8):
                    P.job([(lambda E, ps, lc=lc, tl=tl: E.matmul(
                        out=ps[:, 0:64],
                        lhsT=ckv_bf[:, lc * 1024 + tl * 128: lc * 1024 + (tl + 1) * 128],
                        rhs=self.wukv_vt[:, lc * 64:(lc + 1) * 64],
                        start=(lc == 0), stop=(lc == 7))) for lc in range(8)],
                        lambda E, ps, tl=tl, th=th: E.copy(
                            out=v_t[:, (th * 8 + tl) * 64:(th * 8 + tl + 1) * 64],
                            in_=ps[:, 0:64]))
            a_kv = b.C("act")
            b.wait("vector", "act", a_kv)
            for tb in range(4):
                b.emit("vector", lambda E, tb=tb: E.scalar_tensor_tensor(
                    out=sq_bf[:, tb * 512:(tb + 1) * 512],
                    in0=kc_f32[:, tb * 512:(tb + 1) * 512], scalar=1.0,
                    in1=kc_f32[:, tb * 512:(tb + 1) * 512],
                    op0=ALU.mult, op1=ALU.mult), selfwait=True)
            v_ksq = b.C("dve")
            Pm2 = Pipe(b, [p_ms])
            for tb in range(4):
                Pm2.job([lambda E, ps, tb=tb: E.matmul(
                    out=ps[0:1, 0:512], lhsT=self.ones_bf[0:64, 0:1],
                    rhs=sq_bf[:, tb * 512:(tb + 1) * 512], start=True, stop=True)],
                    lambda E, ps, tb=tb: E.copy(
                        out=msrow[0:1, tb * 512:(tb + 1) * 512], in_=ps[0:1, 0:512]),
                    pre=[("dve", v_ksq), ("act", a_rs)])
            b.emit("scalar", lambda E: E.activation(
                out=lnrow[0:1, :], in_=msrow[0:1, :], func=AF.Ln,
                bias=self.consts[0:1, 1:2], scale=1.0 / HD), selfwait=True)
            b.emit("scalar", lambda E: E.activation(
                out=rstd_row[0:1, :], in_=lnrow[0:1, :], func=AF.Exp,
                bias=0.0, scale=-0.5), selfwait=True)
            a_krs = b.C("act")
            for tb in range(4):
                P.job([lambda E, ps, tb=tb: E.matmul(
                    out=ps[0:64, 0:512], lhsT=self.ones_bf[0:1, 0:64],
                    rhs=rstd_row[0:1, tb * 512:(tb + 1) * 512], start=True, stop=True)],
                    lambda E, ps, tb=tb: E.copy(
                        out=rstd_bc[:, tb * 512:(tb + 1) * 512], in_=ps[0:64, 0:512]),
                    pre=[("act", a_krs)], evac_pre=[("dve", v_qcs)])
            a_kbc = b.C("act")
            b.wait("vector", "act", a_kbc)
            for tb in range(4):
                b.emit("vector", lambda E, tb=tb: E.tensor_mul(
                    out=k_full[0:64, tb * 512:(tb + 1) * 512],
                    in0=kc_f32[:, tb * 512:(tb + 1) * 512],
                    in1=rstd_bc[:, tb * 512:(tb + 1) * 512]), selfwait=(tb == 0))
            for tb in range(4):
                P.job([lambda E, ps, tb=tb, xh=xh, poff=poff: E.matmul(
                    out=ps[0:64, 0:512], lhsT=self.wkrt[poff:poff + 64, 0:64],
                    rhs=xh[:, tb * 512:(tb + 1) * 512], start=True, stop=True)],
                    lambda E, ps, tb=tb: E.copy(
                        out=k_full[64:128, tb * 512:(tb + 1) * 512], in_=ps[0:64, 0:512]))
            v_kdone = b.C("dve")
            a_kdone = b.C("act")

            if MLA_PART < 2:
                b.barrier()
                continue
            prev_rowmax = None
            prev_tpgrp = None
            prev_ctx_ev = None
            for qt in range(NT):
                klen = (qt + 1) * 128
                nkb512 = (klen + 511) // 512
                first_ev_pre = [("dve", prev_rowmax)] if prev_rowmax else []
                for kb in range(nkb512):
                    wdt = min(512, klen - kb * 512)
                    P.job([lambda E, ps, kb=kb, wdt=wdt, qt=qt: E.matmul(
                        out=ps[:, 0:wdt], lhsT=q_full[:, qt * 128:(qt + 1) * 128],
                        rhs=k_full[:, kb * 512: kb * 512 + wdt], start=True, stop=True)],
                        lambda E, ps, kb=kb, wdt=wdt: E.copy(
                            out=z_bf[:, kb * 512: kb * 512 + wdt], in_=ps[:, 0:wdt]),
                        pre=([("dve", v_qdone), ("act", a_kdone), ("dve", v_kdone)]
                             if qt == 0 and kb == 0 else []),
                        evac_pre=(first_ev_pre if kb == 0 else []))
                a_sc = b.C("act")
                b.wait("vector", "act", a_sc)
                b.emit("vector", lambda E, qt=qt: E.tensor_add(
                    out=z_bf[:, qt * 128:(qt + 1) * 128],
                    in0=z_bf[:, qt * 128:(qt + 1) * 128], in1=cmask[:]), selfwait=True)
                b.emit("vector", lambda E, klen=klen: E.tensor_scalar(
                    out=z_bf[:, 0:klen],
                    in0=z_bf[:, 0:klen], scalar1=1.0, scalar2=None,
                    op0=ALU.mult, op1=ALU.max, accum_out=acol[:, 0:1]), selfwait=True)
                b.emit("vector", lambda E: E.tensor_scalar(
                    out=acol[:, 1:2], in0=acol[:, 0:1], scalar1=-1.0, scalar2=None,
                    op0=ALU.mult), selfwait=True)
                v_mask = b.C("dve")
                b.wait("scalar", "dve", v_mask)
                if prev_tpgrp is not None:
                    b.wait("scalar", "pe", prev_tpgrp)
                b.emit("scalar", lambda E, klen=klen: E.activation(
                    out=p_bf[:, 0:klen], in_=z_bf[:, 0:klen], func=AF.Exp,
                    bias=acol[:, 1:2], scale=1.0, accum_out=acol[:, 2:3]), selfwait=True)
                a_exp = b.C("act")
                b.wait("vector", "act", a_exp)
                b.emit("vector", lambda E: E.reciprocal(out=acol[:, 3:4], in_=acol[:, 2:3]),
                       selfwait=True)
                b.emit("vector", lambda E, klen=klen: E.tensor_scalar(
                    out=p_bf[:, 0:klen], in0=p_bf[:, 0:klen], scalar1=acol[:, 3:4],
                    scalar2=None, op0=ALU.mult), selfwait=True)
                v_p = b.C("dve")
                prev_rowmax = v_p
                nkb = qt + 1
                prev_evac = None
                for g0 in range(0, nkb, 8):
                    gn = min(8, nkb - g0)
                    b.wait("tensor", "dve", v_p)
                    if prev_evac is not None:
                        b.wait("tensor", "act", prev_evac)
                    if prev_ctx_ev is not None and g0 == 0:
                        b.wait("tensor", "act", prev_ctx_ev)
                    for j in range(gn):
                        kb = g0 + j
                        b.emit("tensor", lambda E, kb=kb, j=j: E.transpose(
                            out=self.p_tp[:, j * 128:(j + 1) * 128],
                            in_=p_bf[:, kb * 128:(kb + 1) * 128],
                            identity=self.identb[:]), sem="pe" if j == gn - 1 else "")
                    pt = b.C("pe")
                    b.wait("scalar", "pe", pt)
                    b.emit("scalar", lambda E, gn=gn: E.copy(
                        out=pT_sb[:, 0:gn * 128], in_=self.p_tp[:, 0:gn * 128]))
                    prev_evac = b.C("act")
                    b.wait("tensor", "act", prev_evac)
                    for j in range(gn):
                        kb = g0 + j
                        b.emit("tensor", lambda E, kb=kb, j=j, qt=qt: E.matmul(
                            out=p_ctx[0:64, 0:128], lhsT=v_t[:, kb * 64:(kb + 1) * 64],
                            rhs=pT_sb[:, j * 128:(j + 1) * 128],
                            start=(kb == 0), stop=(kb == qt)),
                            sem="pe" if kb == qt else "")
                pt_ctx = b.C("pe")
                prev_tpgrp = pt_ctx
                b.wait("scalar", "pe", pt_ctx)
                b.emit("scalar", lambda E, qt=qt, chunk=chunk, poff=poff: E.copy(
                    out=ctx_all[poff:poff + 64,
                                chunk * T + qt * 128: chunk * T + (qt + 1) * 128],
                    in_=p_ctx[0:64, 0:128]))
                prev_ctx_ev = b.C("act")
            b.barrier()

        pes.close()
        if MLA_PART < 3:
            b.barrier()
            self.tap = dict(ctx_all=ctx_all[:])
            es2.close()
            return
        with nc.psum_tensor("p_ao", [128, 1024], F32) as p_ao:
            prev_ev = {}
            for tt in range(NT):
                par = tt % 2
                if par in prev_ev:
                    b.wait("tensor", "dve", prev_ev[par])
                for c in range(NC_):
                    b.emit("tensor", lambda E, c=c, tt=tt, par=par: E.matmul(
                        out=p_ao[:, par * 512 + (c % 4) * 128: par * 512 + (c % 4 + 1) * 128],
                        lhsT=ctx_all[:, c * T + tt * 128: c * T + (tt + 1) * 128],
                        rhs=self.wot2[:], start=True, stop=True, skip_group_check=True),
                        sem="pe" if c % 4 == 3 else "")
                    if c == 3 or c == 7:
                        pm = b.C("pe")
                        b.wait("vector", "pe", pm)
                        g0 = (c - 3) * 128
                        b.emit("vector", lambda E, tt=tt, par=par, g0=g0: E.tensor_add(
                            out=self.x_acc[:, tt * D + g0: tt * D + g0 + 512],
                            in0=p_ao[:, par * 512:(par + 1) * 512],
                            in1=self.x_acc[:, tt * D + g0: tt * D + g0 + 512]),
                            selfwait=True)
                        prev_ev[par] = b.C("dve")
                        par = (par + 1) % 2
                        if c == 3 and par in prev_ev:
                            b.wait("tensor", "dve", prev_ev[par])
        b.barrier()
        self.tap = dict(ctx_all=ctx_all[:], q_full=q_full[:], k_full=k_full[:],
                        v_t=v_t[:], cq_bf=cq_bf[:], qc_f32=qc_f32[:], z_bf=z_bf[:],
                        p_bf=p_bf[:], rstd_row=rstd_row[:], msrow=msrow[:],
                        wdqt=self.wdqt[:], wuqt=self.wuqt[:], wqrt=self.wqrt[:],
                        wkrt=self.wkrt[:], wot=self.wot[:], svec=self.svec[:],
                        wukv_kt=self.wukv_kt[:], wukv_vt=self.wukv_vt[:])
        es2.close()
        self._es_mw.close()

    def moe0(self):
        b, nc, w = self.b, self.nc, self.w
        es3 = ExitStack()
        logits = es3.enter_context(nc.sbuf_tensor("logits", [128, NT * E8], F32))
        wgt = es3.enter_context(nc.sbuf_tensor("wgt", [128, NT * E8], F32))
        gsc = es3.enter_context(nc.sbuf_tensor("gsc", [128, 16], F32))
        exprow = es3.enter_context(nc.sbuf_tensor("exprow", [128, 8], F32))
        wrT = es3.enter_context(nc.sbuf_tensor("wrT", [128, NC_ * E8], BF16))
        wrT64 = es3.enter_context(nc.sbuf_tensor("wrT64", [128, NC_ * 64], BF16))
        rb_bf = es3.enter_context(nc.sbuf_tensor("rb_bf", [1, E8], BF16))
        onesrow = es3.enter_context(nc.sbuf_tensor("onesrow", [1, 128], BF16))
        dpb_bf = es3.enter_context(nc.sbuf_tensor("dpb_bf", [8, D], BF16))
        gupb_cols = es3.enter_context(nc.sbuf_tensor("gupb_cols", [128, 32], F32))
        gstage = es3.enter_context(nc.sbuf_tensor("gstage", [32, 128], F32))
        rbb = es3.enter_context(nc.sbuf_tensor("rbb", [128, 8], F32))
        stage = es3.enter_context(nc.sbuf_tensor("mstage", [128, 3 * 1024], F32))
        stageb = es3.enter_context(nc.sbuf_tensor("mstageb", [128, 2 * 1024], BF16))
        ring = es3.enter_context(nc.sbuf_tensor("ring", [128, 12 * 1024], BF16))
        a_buf = es3.enter_context(nc.sbuf_tensor("a_buf", [128, 8 * T], BF16))
        gc_buf = es3.enter_context(nc.sbuf_tensor("gc_buf", [128, 512], F32))
        sg_buf = es3.enter_context(nc.sbuf_tensor("sg_buf", [128, 512], F32))
        uc_buf = es3.enter_context(nc.sbuf_tensor("uc_buf", [128, 512], F32))
        wgtT = es3.enter_context(nc.sbuf_tensor("wgtT", [8, NT * 128], BF16))
        p_moe = es3.enter_context(nc.psum_tensor("p_moe", [128, 2048], F32))
        p_sm = es3.enter_context(nc.psum_tensor("p_sm", [128, 256], F32))
        p_sm2 = es3.enter_context(nc.psum_tensor("p_sm2", [64, 1024], F32))

        b.emit("gpsimd", lambda E: E.memset(onesrow[0:1, :], 1.0), sem="gp")

        n1 = b.dma("sync", lambda E: E.dma_start(out=stage[0:8, 0:1024], in_=w["l0f_rw"]))
        n2 = b.dma("sync", lambda E: E.dma_start(out=stage[0:1, 1024:1032], in_=w["l0f_rb"]))
        n3 = b.dma("sync", lambda E: E.dma_start(out=stage[0:8, 2048:3072], in_=w["l0f_dpb"]))
        for nm, v in (n1, n2, n3):
            b.wait("vector", nm, v)
        b.emit("vector", lambda E: E.tensor_copy(out=stageb[0:8, 0:1024],
                                                 in_=stage[0:8, 0:1024]), selfwait=True)
        b.emit("vector", lambda E: E.tensor_copy(out=rb_bf[0:1, :],
                                                 in_=stage[0:1, 1024:1032]), selfwait=True)
        b.emit("vector", lambda E: E.tensor_copy(out=dpb_bf[0:8, :],
                                                 in_=stage[0:8, 2048:3072]), selfwait=True)
        cv = b.C("dve")
        b.wait("tensor", "dve", cv)
        for c in range(NC_):
            b.emit("tensor", lambda E, c=c: E.transpose(
                out=self.p_tp[:, c * 8:(c + 1) * 8],
                in_=stageb[0:8, c * 128:(c + 1) * 128],
                identity=self.identb[0:8, 0:8]), sem="pe" if c == NC_ - 1 else "")
        pt = b.C("pe")
        b.wait("scalar", "pe", pt)
        b.emit("scalar", lambda E: E.copy(out=wrT[:, 0:64], in_=self.p_tp[:, 0:64]))
        a_wrT = b.C("act")
        b.emit("gpsimd", lambda E: E.memset(wrT64[:], 0.0), sem="gp")
        gz = b.C("gp")
        b.wait("vector", "gp", gz)
        b.wait("vector", "act", a_wrT)
        for c in range(NC_):
            b.emit("vector", lambda E, c=c: E.tensor_copy(
                out=wrT64[:, c * 64:c * 64 + 8], in_=wrT[:, c * 8:(c + 1) * 8]),
                selfwait=(c == 0))
        v_wrT64 = b.C("dve")

        if int(os.environ.get("KMK_MOE", "3")) == 11:
            b.barrier()
            es3.close()
            return
        MF = int(os.environ.get("KMK_MF", "15"))
        b.wait("tensor", "act", a_wrT)
        b.wait("tensor", "gp", b.C("gp"))
        if not (MF & 1):
            b.barrier()
            es3.close()
            return
        if not (MF & 2):
            b.barrier()
            es3.close()
            return
        if os.environ.get("KMK_NORB"):
            b.emit("scalar", lambda E: E.copy(out=rbb[0:8, 0:1], in_=self.consts[0:8, 0:1]))
            a_rb = b.C("act")
        else:
            b.emit("tensor", lambda E: E.transpose(
                out=p_sm[0:128, 8:9][0:128, 0:1] if False else p_sm[0:128, 8:9],
                in_=stage[0:1, 1024:1152],
                identity=self.identf[0:1, 0:1]), sem="pe")
            pmr = b.C("pe")
            b.wait("scalar", "pe", pmr)
            b.emit("scalar", lambda E: E.copy(out=rbb[0:8, 0:1], in_=p_sm[0:8, 8:9]))
            a_rb = b.C("act")
        prev_lf = {}
        b.wait("tensor", "dve", v_wrT64)
        for ts4 in range(4):
            par = ts4 % 2
            if par in prev_lf:
                b.wait("tensor", "dve", prev_lf[par])
            for c in range(NC_):
                b.emit("tensor", lambda E, c=c, ts4=ts4, par=par: E.matmul(
                    out=p_sm2[0:64, par * 512:(par + 1) * 512],
                    lhsT=wrT64[:, c * 64:(c + 1) * 64],
                    rhs=self.xnf[:, c * T + ts4 * 512:c * T + (ts4 + 1) * 512],
                    start=(c == 0), stop=(c == NC_ - 1)),
                    sem="pe" if c == NC_ - 1 else "")
            pm = b.C("pe")
            b.wait("vector", "pe", pm)
            b.wait("vector", "act", a_rb)
            b.emit("vector", lambda E, ts4=ts4, par=par: E.tensor_scalar(
                out=stage[0:8, ts4 * 512:(ts4 + 1) * 512],
                in0=p_sm2[0:8, par * 512:(par + 1) * 512],
                scalar1=rbb[0:8, 0:1], scalar2=None, op0=ALU.add), selfwait=True)
            prev_lf[par] = b.C("dve")
        a_lf = b.C("dve")
        if os.environ.get("KMK_L2") == "1":
            b.barrier()
            es3.close()
            return
        b.wait("tensor", "dve", a_lf)
        prev_tt = {}
        for tt in range(NT):
            par = tt % 2
            if par in prev_tt:
                b.wait("tensor", "act", prev_tt[par])
            b.emit("tensor", lambda E, tt=tt, par=par: E.transpose(
                out=p_sm[0:128, par * 8: par * 8 + 8],
                in_=stage[0:8, tt * 128:(tt + 1) * 128],
                identity=self.identf[0:8, 0:8]), sem="pe")
            pm = b.C("pe")
            b.wait("scalar", "pe", pm)
            b.emit("scalar", lambda E, tt=tt, par=par: E.copy(
                out=logits[:, tt * E8:(tt + 1) * E8],
                in_=p_sm[0:128, par * 8: par * 8 + 8]))
            prev_tt[par] = b.C("act")
        a_logits = b.C("act")
        b.wait("vector", "act", a_logits)
        if not (MF & 4):
            b.barrier()
            es3.close()
            return
        for tt in range(NT):
            lg = wgt[:, tt * E8:(tt + 1) * E8]
            lgs = logits[:, tt * E8:(tt + 1) * E8]
            b.emit("vector", lambda E, lgs=lgs: E.max(out=gsc[:, 0:8], in_=lgs),
                   selfwait=True)
            b.emit("vector", lambda E: E.tensor_scalar(
                out=gsc[:, 8:9], in0=gsc[:, 0:1], scalar1=-1.0, scalar2=None,
                op0=ALU.mult), selfwait=True)
            b.emit("vector", lambda E, lg=lg, lgs=lgs: E.tensor_scalar(
                out=lg, in0=lgs, scalar1=gsc[:, 3:4], scalar2=None,
                op0=ALU.is_ge), selfwait=True)
            v_m = b.C("dve")
            b.wait("scalar", "dve", v_m)
            b.emit("scalar", lambda E, lgs=lgs: E.activation(
                out=exprow[:], in_=lgs, func=AF.Exp, bias=gsc[:, 8:9], scale=1.0),
                selfwait=True)
            a_e = b.C("act")
            b.wait("vector", "act", a_e)
            b.emit("vector", lambda E, lg=lg: E.scalar_tensor_tensor(
                out=lg, in0=lg, scalar=1.0, in1=exprow[:], op0=ALU.mult, op1=ALU.mult,
                accum_out=gsc[:, 8:9]), selfwait=True)
            b.emit("vector", lambda E: E.reciprocal(out=gsc[:, 9:10], in_=gsc[:, 8:9]),
                   selfwait=True)
            b.emit("vector", lambda E, lg=lg: E.tensor_scalar(
                out=lg, in0=lg, scalar1=gsc[:, 9:10], scalar2=None, op0=ALU.mult),
                selfwait=True)
        v_gates = b.C("dve")

        if int(os.environ.get("KMK_MOE", "3")) == 13:
            b.barrier()
            es3.close()
            return
        b.wait("tensor", "dve", v_gates)
        prev_wt = None
        for tt in range(NT):
            if prev_wt is not None:
                b.wait("tensor", "act", prev_wt)
            b.emit("tensor", lambda E, tt=tt: E.transpose(
                out=p_sm[0:8, 0:128], in_=wgt[:, tt * E8:(tt + 1) * E8],
                identity=self.identf[:]), sem="pe")
            pm = b.C("pe")
            b.wait("scalar", "pe", pm)
            b.emit("scalar", lambda E, tt=tt: E.copy(
                out=wgtT[0:8, tt * 128:(tt + 1) * 128], in_=p_sm[0:8, 0:128]))
            prev_wt = b.C("act")

        if int(os.environ.get("KMK_MOE", "3")) < 1:
            b.barrier()
            es3.close()
            return
        stage_free = {0: cv, 1: cv, 2: cv}
        stgb_free = {}
        ring_free = {}
        tp_free = [b.C("act")]
        blk_i = [0]

        def prep_block(dram_fn, slot):
            i = blk_i[0]; blk_i[0] += 1
            ss, sb_ = i % 3, i % 2
            if ss in stage_free:
                b.wait("sync", "dve", stage_free[ss])
            nmv = b.dma("sync", lambda E, ss=ss, f=dram_fn: E.dma_start(
                out=stage[:, ss * 1024:(ss + 1) * 1024], in_=f()))
            b.wait("vector", nmv[0], nmv[1])
            if sb_ in stgb_free:
                b.wait("vector", "pe", stgb_free[sb_])
            b.emit("vector", lambda E, ss=ss, sb_=sb_: E.tensor_copy(
                out=stageb[:, sb_ * 1024:(sb_ + 1) * 1024],
                in_=stage[:, ss * 1024:(ss + 1) * 1024]))
            stage_free[ss] = b.C("dve")
            cvv = b.C("dve")
            b.wait("tensor", "dve", cvv)
            b.wait("tensor", "act", tp_free[0])
            for dc in range(NC_):
                b.emit("tensor", lambda E, dc=dc, sb_=sb_: E.transpose(
                    out=self.p_tp[:, dc * 128:(dc + 1) * 128],
                    in_=stageb[:, sb_ * 1024 + dc * 128: sb_ * 1024 + (dc + 1) * 128],
                    identity=self.identb[:]), sem="pe" if dc == NC_ - 1 else "")
            ptv = b.C("pe")
            stgb_free[sb_] = ptv
            b.wait("scalar", "pe", ptv)
            if slot in ring_free:
                b.wait("scalar", "pe", ring_free[slot])
            b.emit("scalar", lambda E, slot=slot: E.copy(
                out=ring[:, slot * 1024:(slot + 1) * 1024], in_=self.p_tp[:, 0:1024]))
            tp_free[0] = b.C("act")
            return b.C("act")

        def prep_down(e, db, sg):
            i = blk_i[0]; blk_i[0] += 1
            ss, sb_ = i % 3, i % 2
            if ss in stage_free:
                b.wait("sync", "dve", stage_free[ss])
            nmv = b.dma("sync", lambda E, ss=ss, e=e, db=db, sg=sg: E.dma_start(
                out=stage[:, ss * 1024:(ss + 1) * 1024],
                in_=w["l0f_dp"][e, db * 128:(db + 1) * 128,
                                sg * 1024:(sg + 1) * 1024]))
            b.wait("vector", nmv[0], nmv[1])
            if sb_ in stgb_free:
                b.wait("vector", "pe", stgb_free[sb_])
            b.emit("vector", lambda E, ss=ss, sb_=sb_: E.tensor_copy(
                out=stageb[:, sb_ * 1024:(sb_ + 1) * 1024],
                in_=stage[:, ss * 1024:(ss + 1) * 1024]))
            stage_free[ss] = b.C("dve")
            cvv = b.C("dve")
            b.wait("tensor", "dve", cvv)
            b.wait("tensor", "act", tp_free[0])
            for c in range(NC_):
                b.emit("tensor", lambda E, c=c, sb_=sb_: E.transpose(
                    out=self.p_tp[:, c * 128:(c + 1) * 128],
                    in_=stageb[:, sb_ * 1024 + c * 128: sb_ * 1024 + (c + 1) * 128],
                    identity=self.identb[:]), sem="pe" if c == NC_ - 1 else "")
            ptv = b.C("pe")
            stgb_free[sb_] = ptv
            b.wait("scalar", "pe", ptv)
            if db == 0:
                for c in range(NC_):
                    if (4 + c) in ring_free:
                        b.wait("scalar", "pe", ring_free[4 + c])
            for c in range(NC_):
                b.emit("scalar", lambda E, c=c, db=db: E.copy(
                    out=ring[:, (4 + c) * 1024 + db * 128: (4 + c) * 1024 + (db + 1) * 128],
                    in_=self.p_tp[:, c * 128:(c + 1) * 128]))
            tp_free[0] = b.C("act")
            return b.C("act")

        b.wait("sync", "pe", b.C("pe"))
        psum_free = {}

        N_EXP = int(os.environ.get("KMK_EXPERTS", str(E8)))
        gstage_free = [None]
        for e in range(N_EXP):
            if gstage_free[0] is not None:
                b.wait("sync", "pe", gstage_free[0])
            nmv = b.dma("sync", lambda E, e=e: E.dma_start(
                out=gstage[:, :],
                in_=w["l0f_gupb"][e:e + 1, :].rearrange("one (c p) -> (one c) p", p=128)))
            b.wait("tensor", nmv[0], nmv[1])
            b.wait("tensor", "act", tp_free[0])
            b.wait("tensor", "dve", b.C("dve"))
            b.emit("tensor", lambda E: E.transpose(
                out=p_sm[0:128, 0:32], in_=gstage[:, :],
                identity=self.identf[0:32, 0:32]), sem="pe")
            ptb = b.C("pe")
            gstage_free[0] = ptb
            b.wait("scalar", "pe", ptb)
            b.emit("scalar", lambda E: E.copy(out=gupb_cols[:, 0:32], in_=p_sm[0:128, 0:32]))
            a_gupb = b.C("act")
            b.wait("vector", "act", a_gupb)

            for sg in range(2):
                down_ready = None
                for fc in range(8):
                    jf = sg * 8 + fc
                    gslot, uslot = (2 * fc) % 4, (2 * fc + 1) % 4
                    prep_block(lambda e=e, jf=jf: w["l0f_gup"][e, jf * 128:(jf + 1) * 128, :],
                               gslot)
                    u_ready = prep_block(
                        lambda e=e, jf=jf: w["l0f_gup"][e, 2048 + jf * 128: 2048 + (jf + 1) * 128, :],
                        uslot)
                    b.wait("tensor", "act", u_ready)
                    for tb in range(4):
                        qg, qu = tb % 2, 2 + tb % 2
                        if qg in psum_free:
                            b.wait("tensor", "dve", psum_free[qg])
                        for dc in range(NC_):
                            b.emit("tensor", lambda E, dc=dc, tb=tb, qg=qg, gslot=gslot: E.matmul(
                                out=p_moe[:, qg * 512:(qg + 1) * 512],
                                lhsT=ring[:, gslot * 1024 + dc * 128: gslot * 1024 + (dc + 1) * 128],
                                rhs=self.xnf[:, dc * T + tb * 512: dc * T + (tb + 1) * 512],
                                start=(dc == 0), stop=(dc == NC_ - 1)), sem="")
                        if qu in psum_free:
                            b.wait("tensor", "dve", psum_free[qu])
                        for dc in range(NC_):
                            b.emit("tensor", lambda E, dc=dc, tb=tb, qu=qu, uslot=uslot: E.matmul(
                                out=p_moe[:, qu * 512:(qu + 1) * 512],
                                lhsT=ring[:, uslot * 1024 + dc * 128: uslot * 1024 + (dc + 1) * 128],
                                rhs=self.xnf[:, dc * T + tb * 512: dc * T + (tb + 1) * 512],
                                start=(dc == 0), stop=(dc == NC_ - 1)),
                                sem="pe" if dc == NC_ - 1 else "")
                        pmu = b.C("pe")
                        b.wait("vector", "pe", pmu)
                        b.emit("vector", lambda E, qg=qg, jf=jf: E.tensor_scalar(
                            out=gc_buf[:], in0=p_moe[:, qg * 512:(qg + 1) * 512],
                            scalar1=gupb_cols[:, jf:jf + 1], scalar2=7.0,
                            op0=ALU.add, op1=ALU.min), selfwait=True)
                        v_gc = b.C("dve")
                        b.wait("scalar", "dve", v_gc)
                        b.emit("scalar", lambda E: E.activation(
                            out=sg_buf[:], in_=gc_buf[:], func=AF.Sigmoid,
                            bias=0.0, scale=1.702), selfwait=True)
                        a_sg = b.C("act")
                        b.emit("vector", lambda E, qu=qu, jf=jf: E.tensor_scalar(
                            out=uc_buf[:], in0=p_moe[:, qu * 512:(qu + 1) * 512],
                            scalar1=gupb_cols[:, 16 + jf:16 + jf + 1], scalar2=7.0,
                            op0=ALU.add, op1=ALU.min), selfwait=True)
                        psum_free[qu] = b.C("dve")
                        b.emit("vector", lambda E: E.tensor_scalar(
                            out=uc_buf[:], in0=uc_buf[:], scalar1=-7.0, scalar2=1.0,
                            op0=ALU.max, op1=ALU.add), selfwait=True)
                        b.wait("vector", "act", a_sg)
                        b.emit("vector", lambda E: E.tensor_mul(
                            out=gc_buf[:], in0=gc_buf[:], in1=sg_buf[:]), selfwait=True)
                        psum_free[qg] = b.C("dve")
                        b.emit("vector", lambda E, fc=fc, tb=tb: E.tensor_mul(
                            out=a_buf[:, fc * T + tb * 512: fc * T + (tb + 1) * 512],
                            in0=gc_buf[:], in1=uc_buf[:]), selfwait=True)
                    ring_free[gslot] = b.C("pe")
                    ring_free[uslot] = b.C("pe")
                    down_ready = prep_down(e, fc, sg)
                v_a = b.C("dve")
                b.wait("tensor", "dve", v_a)
                b.wait("tensor", "act", down_ready)
                for tt in range(NT):
                    par = tt % 2
                    k0, k1 = 2 * par, 2 * par + 1
                    if k0 in psum_free:
                        b.wait("tensor", "dve", psum_free[k0])
                    if k1 in psum_free:
                        b.wait("tensor", "dve", psum_free[k1])
                    for ic in range(8):
                        for dh in range(2):
                            b.emit("tensor", lambda E, ic=ic, dh=dh, tt=tt, par=par: E.matmul(
                                out=p_moe[:, par * 1024 + dh * 512: par * 1024 + (dh + 1) * 512],
                                lhsT=a_buf[:, ic * T + tt * 128: ic * T + (tt + 1) * 128],
                                rhs=ring[:, (4 + ic) * 1024 + dh * 512: (4 + ic) * 1024 + (dh + 1) * 512],
                                start=(ic == 0), stop=(ic == 7)),
                                sem="pe" if (ic == 7 and dh == 1) else "")
                    pmo = b.C("pe")
                    b.wait("vector", "pe", pmo)
                    b.emit("vector", lambda E, tt=tt, par=par, e=e: E.scalar_tensor_tensor(
                        out=self.x_acc[:, tt * D:(tt + 1) * D],
                        in0=p_moe[:, par * 1024:(par + 1) * 1024],
                        scalar=wgt[:, tt * E8 + e: tt * E8 + e + 1],
                        in1=self.x_acc[:, tt * D:(tt + 1) * D],
                        op0=ALU.mult, op1=ALU.add), selfwait=True)
                    ve = b.C("dve")
                    psum_free[k0] = ve
                    psum_free[k1] = ve
                for c in range(NC_):
                    ring_free[4 + c] = b.C("pe")

        if int(os.environ.get("KMK_MOE", "3")) < 3:
            b.barrier()
            es3.close()
            return
        prev = None
        for tt in range(NT):
            par = tt % 2
            k0, k1 = 2 * par, 2 * par + 1
            if k0 in psum_free:
                b.wait("tensor", "dve", psum_free[k0])
            if k1 in psum_free:
                b.wait("tensor", "dve", psum_free[k1])
            for dh in range(2):
                b.emit("tensor", lambda E, tt=tt, dh=dh, par=par: E.matmul(
                    out=p_moe[:, par * 1024 + dh * 512: par * 1024 + (dh + 1) * 512],
                    lhsT=wgtT[0:8, tt * 128:(tt + 1) * 128],
                    rhs=dpb_bf[0:8, dh * 512:(dh + 1) * 512],
                    start=True, stop=True), sem="pe" if dh == 1 else "")
            pm = b.C("pe")
            b.wait("vector", "pe", pm)
            b.emit("vector", lambda E, tt=tt, par=par: E.tensor_add(
                out=self.x_acc[:, tt * D:(tt + 1) * D],
                in0=p_moe[:, par * 1024:(par + 1) * 1024],
                in1=self.x_acc[:, tt * D:(tt + 1) * D]), selfwait=True)
            ve = b.C("dve")
            psum_free[k0] = ve
            psum_free[k1] = ve
        b.barrier()
        es3.close()

    def l1_ln_spill(self):
        b = self.b
        self.ln1_rstd, self.ln1_negmr = self.ln_to_xnf(self.x_acc)
        b.wait("gpsimd", "dve", b.C("dve"))
        self.spill_sems = []
        for tt in range(NT):
            nmv = b.dma("gpsimd", lambda E, tt=tt: E.dma_start(
                out=self.out[tt * 128:(tt + 1) * 128, :],
                in_=self.x_acc[:, tt * D:(tt + 1) * D]))
            self.spill_sems.append(nmv)
        b.barrier()

    def l1_ffn_attn(self):
        b, nc, w = self.b, self.nc, self.w
        PAIRS = [(0, 4), (1, 5), (2, 6), (3, 7), (8, 12), (9, 13), (10, 14), (11, 15)]
        es4 = ExitStack()
        ln1c = es4.enter_context(nc.sbuf_tensor("ln1c", [128, 8], F32))
        cos_t = es4.enter_context(nc.sbuf_tensor("cos_t", [128, NT * 32], F32))
        sin_t = es4.enter_context(nc.sbuf_tensor("sin_t", [128, NT * 32], F32))
        wqf = es4.enter_context(nc.sbuf_tensor("wqf", [128, NC_ * D], BF16))
        wkf = es4.enter_context(nc.sbuf_tensor("wkf", [128, NC_ * 256], BF16))
        wvf = es4.enter_context(nc.sbuf_tensor("wvf", [128, NC_ * 256], BF16))
        wof = es4.enter_context(nc.sbuf_tensor("wof", [128, NC_ * D], BF16))
        qf = es4.enter_context(nc.sbuf_tensor("qf", [128, 8 * T], BF16))
        kf = es4.enter_context(nc.sbuf_tensor("kf", [128, 2 * T], BF16))
        vt = es4.enter_context(nc.sbuf_tensor("vt", [128, NT * 256], BF16))
        stg = es4.enter_context(nc.sbuf_tensor("stg", [128, 2 * 1024], F32))
        qt32 = es4.enter_context(nc.sbuf_tensor("qt32", [128, 1024], F32))
        rs1 = stg[:, 0:512]
        rs2 = stg[:, 512:1024]
        kt32 = stg[:, 1024:1280]
        qrot = es4.enter_context(nc.sbuf_tensor("qrot", [128, 1024], BF16))
        krot = es4.enter_context(nc.sbuf_tensor("krot", [128, 256], BF16))
        z_bf = wqf[:, 0:T]
        p_bf = wqf[:, T:2 * T]
        pT_sb = wqf[:, 2 * T:2 * T + 1024]
        ctxp = wqf[:, 2 * T + 1024:2 * T + 1152]
        cmask = wqf[:, 2 * T + 1152:2 * T + 1280]
        acol = stg[:, 1280:1288]
        pes = ExitStack()
        pA = pes.enter_context(nc.psum_tensor("pA", [128, 512], F32))
        pB = pes.enter_context(nc.psum_tensor("pB", [128, 512], F32))
        pC = pes.enter_context(nc.psum_tensor("pC", [128, 512], F32))
        pD = pes.enter_context(nc.psum_tensor("pD", [128, 512], F32))
        p_ctx = pes.enter_context(nc.psum_tensor("p_ctx1", [64, 128], F32))
        p_f32a = pes.enter_context(nc.psum_tensor("p_f32a", [128, 64], F32))

        b.emit("gpsimd", lambda E: E.memset(cmask[:], 0.0))
        b.emit("gpsimd", lambda E: E.affine_select(
            out=cmask[:], in_=cmask[:], compare_op=ALU.is_ge, fill=-1e9,
            base=0, pattern=[[-1, 128]], channel_multiplier=1), selfwait=True)
        gp_cm = b.C("gp")
        b.wait("vector", "gp", gp_cm)

        n1 = b.dma("sync", lambda E: E.dma_start(
            out=stg[0:8, 0:128],
            in_=w["l1f_ln1"][0:1, :].rearrange("one (c p) -> (one c) p", p=128)))
        b.wait("tensor", n1[0], n1[1])
        b.wait("tensor", "act", b.C("act"))
        b.wait("tensor", "dve", b.C("dve"))
        b.emit("tensor", lambda E: E.transpose(
            out=p_f32a[0:128, 0:8], in_=stg[0:8, 0:128],
            identity=self.identf[0:8, 0:8]), sem="pe")
        ptv = b.C("pe")
        b.wait("scalar", "pe", ptv)
        b.wait("sync", "pe", ptv)
        b.emit("scalar", lambda E: E.copy(out=ln1c[:, 0:8], in_=p_f32a[0:128, 0:8]))
        a_ln1 = b.C("act")

        last = None
        for tt in range(NT):
            b.dma("sync", lambda E, tt=tt: E.dma_start(
                out=cos_t[:, tt * 32:(tt + 1) * 32],
                in_=w["rope_cos"][tt * 128:(tt + 1) * 128, :]))
            last = b.dma("sync", lambda E, tt=tt: E.dma_start(
                out=sin_t[:, tt * 32:(tt + 1) * 32],
                in_=w["rope_sin"][tt * 128:(tt + 1) * 128, :]))
        for i in range(6):
            nm = f"dp{i}"
            b.wait("vector", nm, b.C(nm))

        def load_fold(dst, dram, ncols, fold):
            for dc in range(NC_):
                sslot = dc % 2
                b.wait("sync", "dve", b.C("dve"))
                nmv = b.dma("sync", lambda E, dc=dc, sslot=sslot, dram=dram, ncols=ncols:
                            E.dma_start(out=stg[:, sslot * 1024: sslot * 1024 + ncols],
                                        in_=dram[dc * 128:(dc + 1) * 128, :]))
                b.wait("vector", nmv[0], nmv[1])
                if fold:
                    b.wait("vector", "act", a_ln1)
                    b.emit("vector", lambda E, dc=dc, sslot=sslot, dst=dst, ncols=ncols:
                           E.tensor_scalar(
                               out=dst[:, dc * ncols:(dc + 1) * ncols],
                               in0=stg[:, sslot * 1024: sslot * 1024 + ncols],
                               scalar1=ln1c[:, dc:dc + 1], scalar2=None, op0=ALU.mult),
                           selfwait=True)
                else:
                    b.emit("vector", lambda E, dc=dc, sslot=sslot, dst=dst, ncols=ncols:
                           E.tensor_copy(
                               out=dst[:, dc * ncols:(dc + 1) * ncols],
                               in_=stg[:, sslot * 1024: sslot * 1024 + ncols]),
                           selfwait=True)

        load_fold(wqf, w["l1f_wq"], 1024, True)
        load_fold(wkf, w["l1f_wk"], 256, True)
        load_fold(wvf, w["l1f_wv"], 256, True)
        for tau, (hA, hB) in enumerate(PAIRS):
            for half, h in ((0, hA), (1, hB)):
                sslot = half % 2
                b.wait("sync", "dve", b.C("dve"))
                nmv = b.dma("sync", lambda E, h=h, sslot=sslot: E.dma_start(
                    out=stg[0:64, sslot * 1024: sslot * 1024 + 1024],
                    in_=w["l1f_wo"][h * 64:(h + 1) * 64, :]))
                b.wait("vector", nmv[0], nmv[1])
                b.emit("vector", lambda E, tau=tau, half=half, sslot=sslot: E.tensor_copy(
                    out=wof[64 * half:64 * half + 64, tau * D:(tau + 1) * D],
                    in_=stg[0:64, sslot * 1024: sslot * 1024 + 1024]), selfwait=True)
        v_w = b.C("dve")
        b.wait("tensor", "dve", v_w)

        rstd1, negmr1 = self.ln1_rstd, self.ln1_negmr
        for tt in range(NT):
            nm, v = self.spill_sems[tt]
            b.wait("vector", nm, v)
            xt = self.x_acc[:, tt * D:(tt + 1) * D]
            b.emit("vector", lambda E, xt=xt, tt=tt: E.tensor_scalar(
                out=xt, in0=xt, scalar1=rstd1[:, tt:tt + 1],
                scalar2=negmr1[:, tt:tt + 1], op0=ALU.mult, op1=ALU.add), selfwait=True)
            b.emit("vector", lambda E, xt=xt, tt=tt: E.scalar_tensor_tensor(
                out=self.sjunk[:, 0:1024], in0=xt, scalar=1.0, in1=xt,
                op0=ALU.mult, op1=ALU.mult,
                accum_out=self.stats[:, 48 + tt:48 + tt + 1]), selfwait=True)
        v_h1b = b.C("dve")
        b.wait("scalar", "dve", v_h1b)
        b.emit("scalar", lambda E: E.activation(
            out=self.stats[:, 32:48], in_=self.stats[:, 48:64], func=AF.Ln,
            bias=self.consts[:, 0:1], scale=1.0 / D), selfwait=True)
        b.emit("scalar", lambda E: E.activation(
            out=self.stats[:, 0:16], in_=self.stats[:, 32:48], func=AF.Exp,
            bias=0.0, scale=-0.5), selfwait=True)
        a_rms1 = b.C("act")
        rms1 = self.stats[:, 0:16]
        b.wait("vector", "act", a_rms1)
        h1 = self.x_acc

        rowof = {}
        for tau, (hA, hB) in enumerate(PAIRS):
            rowof[hA] = (tau, 0)
            rowof[hB] = (tau, 64)
        ev = {}
        prev_tpev = None
        for tt in range(NT):
            for dh in range(2):
                ps = (pA, pB)[dh]
                if ("q", dh) in ev:
                    b.wait("tensor", "dve", ev[("q", dh)])
                for dc in range(NC_):
                    b.emit("tensor", lambda E, dc=dc, dh=dh, tt=tt, ps=ps: E.matmul(
                        out=ps[:, 0:512],
                        lhsT=self.xnf[:, dc * T + tt * 128: dc * T + (tt + 1) * 128],
                        rhs=wqf[:, dc * 1024 + dh * 512: dc * 1024 + (dh + 1) * 512],
                        start=(dc == 0), stop=(dc == NC_ - 1)),
                        sem="pe" if dc == NC_ - 1 else "")
                pm = b.C("pe")
                b.wait("vector", "pe", pm)
                b.emit("vector", lambda E, dh=dh, tt=tt, ps=ps: E.tensor_scalar(
                    out=qt32[:, dh * 512:(dh + 1) * 512], in0=ps[:, 0:512],
                    scalar1=rms1[:, tt:tt + 1], scalar2=None, op0=ALU.mult),
                    selfwait=True)
                ev[("q", dh)] = b.C("dve")
            for nm_, wsrc, ps in (("k", wkf, pC), ("v", wvf, pD)):
                if ("kv", nm_) in ev:
                    b.wait("tensor", "dve", ev[("kv", nm_)])
                for dc in range(NC_):
                    b.emit("tensor", lambda E, dc=dc, tt=tt, wsrc=wsrc, ps=ps: E.matmul(
                        out=ps[:, 0:256],
                        lhsT=self.xnf[:, dc * T + tt * 128: dc * T + (tt + 1) * 128],
                        rhs=wsrc[:, dc * 256:(dc + 1) * 256],
                        start=(dc == 0), stop=(dc == NC_ - 1)),
                        sem="pe" if dc == NC_ - 1 else "")
                pm = b.C("pe")
                b.wait("vector", "pe", pm)
                if nm_ == "k":
                    b.emit("vector", lambda E, tt=tt, ps=ps: E.tensor_scalar(
                        out=kt32[:, 0:256], in0=ps[:, 0:256],
                        scalar1=rms1[:, tt:tt + 1], scalar2=None, op0=ALU.mult),
                        selfwait=True)
                else:
                    b.emit("vector", lambda E, tt=tt, ps=ps: E.tensor_scalar(
                        out=vt[:, tt * 256:(tt + 1) * 256], in0=ps[:, 0:256],
                        scalar1=rms1[:, tt:tt + 1], scalar2=None, op0=ALU.mult),
                        selfwait=True)
                ev[("kv", nm_)] = b.C("dve")
            qv = qt32[:].rearrange("p (h two i) -> p h two i", two=2, i=32)
            ov = qrot[:].rearrange("p (h two i) -> p h two i", two=2, i=32)
            cosb = cos_t[:, tt * 32:(tt + 1) * 32].rearrange(
                "p (one i) -> p one i", one=1).to_broadcast([128, 16, 32])
            sinb = sin_t[:, tt * 32:(tt + 1) * 32].rearrange(
                "p (one i) -> p one i", one=1).to_broadcast([128, 16, 32])
            s1 = rs1[:].rearrange("p (h i) -> p h i", i=32)
            s2 = rs2[:].rearrange("p (h i) -> p h i", i=32)
            b.emit("vector", lambda E, qv=qv, cosb=cosb, s1=s1: E.tensor_mul(
                out=s1, in0=qv[:, :, 0, :], in1=cosb), selfwait=True)
            b.emit("vector", lambda E, qv=qv, sinb=sinb, s2=s2: E.tensor_mul(
                out=s2, in0=qv[:, :, 1, :], in1=sinb), selfwait=True)
            b.emit("vector", lambda E, ov=ov, s1=s1, s2=s2: E.tensor_sub(
                out=ov[:, :, 0, :], in0=s1, in1=s2), selfwait=True)
            b.emit("vector", lambda E, qv=qv, cosb=cosb, s1=s1: E.tensor_mul(
                out=s1, in0=qv[:, :, 1, :], in1=cosb), selfwait=True)
            b.emit("vector", lambda E, qv=qv, sinb=sinb, s2=s2: E.tensor_mul(
                out=s2, in0=qv[:, :, 0, :], in1=sinb), selfwait=True)
            b.emit("vector", lambda E, ov=ov, s1=s1, s2=s2: E.tensor_add(
                out=ov[:, :, 1, :], in0=s1, in1=s2), selfwait=True)
            kv_ = kt32[:, 0:256].rearrange("p (h two i) -> p h two i", two=2, i=32)
            kov = krot[:, 0:256].rearrange("p (h two i) -> p h two i", two=2, i=32)
            cosb4 = cos_t[:, tt * 32:(tt + 1) * 32].rearrange(
                "p (one i) -> p one i", one=1).to_broadcast([128, 4, 32])
            sinb4 = sin_t[:, tt * 32:(tt + 1) * 32].rearrange(
                "p (one i) -> p one i", one=1).to_broadcast([128, 4, 32])
            s14 = rs1[:, 0:128].rearrange("p (h i) -> p h i", i=32)
            s24 = rs2[:, 0:128].rearrange("p (h i) -> p h i", i=32)
            b.emit("vector", lambda E, kv_=kv_, cosb4=cosb4, s14=s14: E.tensor_mul(
                out=s14, in0=kv_[:, :, 0, :], in1=cosb4), selfwait=True)
            b.emit("vector", lambda E, kv_=kv_, sinb4=sinb4, s24=s24: E.tensor_mul(
                out=s24, in0=kv_[:, :, 1, :], in1=sinb4), selfwait=True)
            b.emit("vector", lambda E, kov=kov, s14=s14, s24=s24: E.tensor_sub(
                out=kov[:, :, 0, :], in0=s14, in1=s24), selfwait=True)
            b.emit("vector", lambda E, kv_=kv_, cosb4=cosb4, s14=s14: E.tensor_mul(
                out=s14, in0=kv_[:, :, 1, :], in1=cosb4), selfwait=True)
            b.emit("vector", lambda E, kv_=kv_, sinb4=sinb4, s24=s24: E.tensor_mul(
                out=s24, in0=kv_[:, :, 0, :], in1=sinb4), selfwait=True)
            b.emit("vector", lambda E, kov=kov, s14=s14, s24=s24: E.tensor_add(
                out=kov[:, :, 1, :], in0=s14, in1=s24), selfwait=True)
            v_rope = b.C("dve")
            b.wait("tensor", "dve", v_rope)
            for rnd in range(3):
                if prev_tpev is not None:
                    b.wait("tensor", "act", prev_tpev)
                blocks = ([(h, "q") for h in range(rnd * 8, rnd * 8 + 8)]
                          if rnd < 2 else [(j, "k") for j in range(4)])
                for i2, (bk, kind) in enumerate(blocks):
                    srcb = qrot if kind == "q" else krot
                    b.emit("tensor", lambda E, bk=bk, i2=i2, srcb=srcb: E.transpose(
                        out=self.p_tp[0:64, i2 * 128:(i2 + 1) * 128],
                        in_=srcb[:, bk * 64:(bk + 1) * 64],
                        identity=self.identb[:]), sem="pe" if i2 == len(blocks) - 1 else "")
                pmt = b.C("pe")
                b.wait("scalar", "pe", pmt)
                for i2, (bk, kind) in enumerate(blocks):
                    if kind == "q":
                        tau, roff = rowof[bk]
                        b.emit("scalar", lambda E, bk=bk, i2=i2, tau=tau, roff=roff, tt=tt:
                               E.copy(
                            out=qf[roff:roff + 64,
                                   tau * T + tt * 128: tau * T + (tt + 1) * 128],
                            in_=self.p_tp[0:64, i2 * 128:(i2 + 1) * 128]))
                    else:
                        ktile, kroff = bk // 2, 64 * (bk % 2)
                        b.emit("scalar", lambda E, bk=bk, i2=i2, ktile=ktile, kroff=kroff,
                               tt=tt: E.copy(
                            out=kf[kroff:kroff + 64,
                                   ktile * T + tt * 128: ktile * T + (tt + 1) * 128],
                            in_=self.p_tp[0:64, i2 * 128:(i2 + 1) * 128]))
                prev_tpev = b.C("act")
        a_proj = b.C("act")
        v_proj = b.C("dve")
        ev[("wo", 0)] = v_proj
        ev[("wo", 1)] = v_proj

        prev_rm = None
        prev_tp2 = None
        prev_ctxev = None
        P = Pipe(b, [pA, pB])
        P.last_ev[0] = ("dve", v_proj)
        P.last_ev[1] = ("dve", v_proj)
        for tau, (hA, hB) in enumerate(PAIRS):
            for qt in range(NT):
                for half, h in ((0, hA), (1, hB)):
                    roff = 64 * half
                    ktile = h // 8
                    klen = (qt + 1) * 128
                    nkb = (klen + 511) // 512
                    fep = [("dve", prev_rm)] if prev_rm else []
                    for kb in range(nkb):
                        wdt = min(512, klen - kb * 512)
                        P.job([lambda E, ps, kb=kb, wdt=wdt, qt=qt, tau=tau, roff=roff,
                               ktile=ktile: E.matmul(
                            out=ps[roff:roff + 64, 0:wdt][0:64, 0:wdt] if False else ps[:, 0:wdt],
                            lhsT=qf[roff:roff + 64,
                                    tau * T + qt * 128: tau * T + (qt + 1) * 128],
                            rhs=kf[roff:roff + 64, ktile * T + kb * 512:
                                   ktile * T + kb * 512 + wdt],
                            start=True, stop=True)],
                            lambda E, ps, kb=kb, wdt=wdt: E.activation(
                                out=z_bf[:, kb * 512: kb * 512 + wdt],
                                in_=ps[0:128, 0:wdt], func=AF.Copy, bias=0.0,
                                scale=0.125),
                            pre=([("act", a_proj), ("dve", v_proj)]
                                 if tau == 0 and half == 0 and qt == 0 and kb == 0 else []),
                            evac_pre=(fep if kb == 0 else []))
                    a_sc = b.C("act")
                    b.wait("vector", "act", a_sc)
                    b.emit("vector", lambda E, qt=qt: E.tensor_add(
                        out=z_bf[:, qt * 128:(qt + 1) * 128],
                        in0=z_bf[:, qt * 128:(qt + 1) * 128], in1=cmask[:]), selfwait=True)
                    b.emit("vector", lambda E, klen=klen: E.tensor_scalar(
                        out=z_bf[:, 0:klen], in0=z_bf[:, 0:klen], scalar1=1.0,
                        scalar2=None, op0=ALU.mult, op1=ALU.max,
                        accum_out=acol[:, 0:1]), selfwait=True)
                    b.emit("vector", lambda E: E.tensor_scalar(
                        out=acol[:, 1:2], in0=acol[:, 0:1], scalar1=-1.0, scalar2=None,
                        op0=ALU.mult), selfwait=True)
                    v_mask = b.C("dve")
                    b.wait("scalar", "dve", v_mask)
                    if prev_tp2 is not None:
                        b.wait("scalar", "pe", prev_tp2)
                    b.emit("scalar", lambda E, klen=klen: E.activation(
                        out=p_bf[:, 0:klen], in_=z_bf[:, 0:klen], func=AF.Exp,
                        bias=acol[:, 1:2], scale=1.0, accum_out=acol[:, 2:3]),
                        selfwait=True)
                    a_exp = b.C("act")
                    b.wait("vector", "act", a_exp)
                    b.emit("vector", lambda E: E.reciprocal(out=acol[:, 3:4],
                                                            in_=acol[:, 2:3]), selfwait=True)
                    b.emit("vector", lambda E, klen=klen: E.tensor_scalar(
                        out=p_bf[:, 0:klen], in0=p_bf[:, 0:klen], scalar1=acol[:, 3:4],
                        scalar2=None, op0=ALU.mult), selfwait=True)
                    v_p = b.C("dve")
                    prev_rm = v_p
                    nkb128 = qt + 1
                    prev_ev8 = None
                    for g0 in range(0, nkb128, 8):
                        gn = min(8, nkb128 - g0)
                        b.wait("tensor", "dve", v_p)
                        if prev_ev8 is not None:
                            b.wait("tensor", "act", prev_ev8)
                        if prev_ctxev is not None and g0 == 0:
                            b.wait("tensor", "act", prev_ctxev)
                        for j in range(gn):
                            kb = g0 + j
                            b.emit("tensor", lambda E, kb=kb, j=j: E.transpose(
                                out=self.p_tp[:, j * 128:(j + 1) * 128],
                                in_=p_bf[:, kb * 128:(kb + 1) * 128],
                                identity=self.identb[:]), sem="pe" if j == gn - 1 else "")
                        pt2 = b.C("pe")
                        b.wait("scalar", "pe", pt2)
                        b.emit("scalar", lambda E, gn=gn: E.copy(
                            out=pT_sb[:, 0:gn * 128], in_=self.p_tp[:, 0:gn * 128]))
                        prev_ev8 = b.C("act")
                        b.wait("tensor", "act", prev_ev8)
                        for j in range(gn):
                            kb = g0 + j
                            kvh = h // 4
                            b.emit("tensor", lambda E, kb=kb, j=j, qt=qt, kvh=kvh: E.matmul(
                                out=p_ctx[0:64, 0:128],
                                lhsT=vt[:, kb * 256 + kvh * 64: kb * 256 + (kvh + 1) * 64],
                                rhs=pT_sb[:, j * 128:(j + 1) * 128],
                                start=(kb == 0), stop=(kb == qt)),
                                sem="pe" if kb == qt else "")
                    pt_ctx = b.C("pe")
                    prev_tp2 = pt_ctx
                    b.wait("scalar", "pe", pt_ctx)
                    b.emit("scalar", lambda E, roff=roff, qt=qt: E.copy(
                        out=ctxp[roff:roff + 64, 0:128], in_=p_ctx[0:64, 0:128]))
                    prev_ctxev = b.C("act")
                    if half == 1:
                        for dh in range(2):
                            ps2 = (pC, pD)[dh]
                            key = ("wo", dh)
                            if key in ev:
                                b.wait("tensor", "dve", ev[key])
                            b.wait("tensor", "act", prev_ctxev)
                            b.emit("tensor", lambda E, tau=tau, qt=qt, dh=dh, ps2=ps2:
                                   E.matmul(
                                out=ps2[:, 0:512],
                                lhsT=ctxp[:, 0:128],
                                rhs=wof[:, tau * D + dh * 512: tau * D + (dh + 1) * 512],
                                start=True, stop=True), sem="pe")
                            pmw = b.C("pe")
                            b.wait("vector", "pe", pmw)
                            if tau == 0:
                                b.wait("vector", "dve", v_h1b)
                            b.emit("vector", lambda E, qt=qt, dh=dh, ps2=ps2: E.tensor_add(
                                out=h1[:, qt * D + dh * 512: qt * D + (dh + 1) * 512],
                                in0=ps2[:, 0:512],
                                in1=h1[:, qt * D + dh * 512: qt * D + (dh + 1) * 512]),
                                selfwait=True)
                            ev[key] = b.C("dve")
        self.tap = dict(qf=qf[:], kf=kf[:], vt=vt[:], wqf=wqf[:], cos_t=cos_t[:],
                        ln1c=ln1c[:, 0:8])
        for tt in range(NT):
            xt = h1[:, tt * D:(tt + 1) * D]
            b.emit("vector", lambda E, xt=xt: E.tensor_scalar(
                out=xt, in0=xt, scalar1=100.0, scalar2=-100.0,
                op0=ALU.min, op1=ALU.max), selfwait=True)
            b.emit("vector", lambda E, xt=xt, tt=tt: E.scalar_tensor_tensor(
                out=self.sjunk[:, 0:1024], in0=xt, scalar=1.0, in1=xt,
                op0=ALU.mult, op1=ALU.mult,
                accum_out=self.stats[:, 32 + tt:32 + tt + 1]), selfwait=True)
        v_ms2 = b.C("dve")
        b.wait("scalar", "dve", v_ms2)
        b.emit("scalar", lambda E: E.activation(
            out=self.stats[:, 48:64], in_=self.stats[:, 32:48], func=AF.Ln,
            bias=self.consts[:, 0:1], scale=1.0 / D), selfwait=True)
        b.emit("scalar", lambda E: E.activation(
            out=self.stats[:, 32:48], in_=self.stats[:, 48:64], func=AF.Exp,
            bias=0.0, scale=-0.5), selfwait=True)
        self.rstd2 = self.stats[:, 32:48]
        b.barrier()
        pes.close()
        es4.close()

    def l1_sparsemax(self):
        b, nc, w = self.b, self.nc, self.w
        es5 = ExitStack()
        a1acc = es5.enter_context(nc.sbuf_tensor("a1acc", [128, NT * D], F32))
        stg = es5.enter_context(nc.sbuf_tensor("stg5", [128, 1312], F32))
        stgb = es5.enter_context(nc.sbuf_tensor("stgb5", [128, 1024], BF16))
        wqT = es5.enter_context(nc.sbuf_tensor("wqT5", [128, 1024], BF16))
        wkT = es5.enter_context(nc.sbuf_tensor("wkT5", [128, 1024], BF16))
        wvT = es5.enter_context(nc.sbuf_tensor("wvT5", [128, 1024], BF16))
        woT = es5.enter_context(nc.sbuf_tensor("woT5", [128, 1024], BF16))
        qp = es5.enter_context(nc.sbuf_tensor("qp5", [128, T], BF16))
        kp = es5.enter_context(nc.sbuf_tensor("kp5", [128, T], BF16))
        vp = es5.enter_context(nc.sbuf_tensor("vp5", [128, NT * 128], BF16))
        z_bf = es5.enter_context(nc.sbuf_tensor("z_bf5", [128, T], BF16))
        zeros = es5.enter_context(nc.sbuf_tensor("zeros5", [128, T], BF16))
        p_bf = es5.enter_context(nc.sbuf_tensor("p_bf5", [128, T], BF16))
        zj = p_bf
        pT_sb = zeros[:, 0:1024][0:128, 0:1024] if False else es5.enter_context(
            nc.sbuf_tensor("pT_sb5", [128, 1024], BF16))
        ctxp = es5.enter_context(nc.sbuf_tensor("ctxp5", [128, 128], BF16))
        cmask = es5.enter_context(nc.sbuf_tensor("cmask5", [128, 128], BF16))
        acol = stg[:, 1280:1288]
        pes = ExitStack()
        pA = pes.enter_context(nc.psum_tensor("pA5", [128, 512], F32))
        pB = pes.enter_context(nc.psum_tensor("pB5", [128, 512], F32))
        pC = pes.enter_context(nc.psum_tensor("pC5", [128, 512], F32))
        pD = pes.enter_context(nc.psum_tensor("pD5", [128, 512], F32))
        p_ctx = pes.enter_context(nc.psum_tensor("p_ctx5", [64, 128], F32))

        b.emit("gpsimd", lambda E: E.memset(cmask[:], 0.0))
        b.emit("gpsimd", lambda E: E.affine_select(
            out=cmask[:], in_=cmask[:], compare_op=ALU.is_ge, fill=-1e9,
            base=0, pattern=[[-1, 128]], channel_multiplier=1), selfwait=True)
        b.emit("gpsimd", lambda E: E.memset(zeros[:], 0.0), sem="gp")
        gp_cm = b.C("gp")
        b.wait("vector", "gp", gp_cm)
        b.wait("tensor", "gp", gp_cm)

        def prepT(dst, rows_ap, name):
            b.wait("sync", "dve", b.C("dve"))
            nmv = b.dma("sync", lambda E, rows_ap=rows_ap: E.dma_start(
                out=stg[:, 0:1024], in_=rows_ap))
            b.wait("vector", nmv[0], nmv[1])
            b.wait("vector", "pe", b.C("pe"))
            b.emit("vector", lambda E: E.tensor_copy(out=stgb[:, 0:1024],
                                                     in_=stg[:, 0:1024]), selfwait=True)
            cv = b.C("dve")
            b.wait("tensor", "dve", cv)
            b.wait("tensor", "act", b.C("act"))
            for dc in range(NC_):
                b.emit("tensor", lambda E, dc=dc: E.transpose(
                    out=self.p_tp[:, dc * 128:(dc + 1) * 128],
                    in_=stgb[:, dc * 128:(dc + 1) * 128],
                    identity=self.identb[:]), sem="pe" if dc == NC_ - 1 else "")
            pt = b.C("pe")
            b.wait("scalar", "pe", pt)
            b.emit("scalar", lambda E, dst=dst: E.copy(out=dst[:, 0:1024],
                                                       in_=self.p_tp[:, 0:1024]))
            return b.C("act")

        P = Pipe(b, [pA, pB])
        prev_rm = None
        prev_tp2 = None
        prev_ctxev = None
        ev = {}
        for pr in range(8):
            a_q = prepT(wqT, w["l1a_wq"][pr * 128:(pr + 1) * 128, :], "q")
            a_k = prepT(wkT, w["l1a_wk"][pr * 128:(pr + 1) * 128, :], "k")
            a_v = prepT(wvT, w["l1a_wv"][pr * 128:(pr + 1) * 128, :], "v")
            nmvs = []
            for eb in range(NC_):
                sslot = eb % 2
                b.wait("sync", "dve", b.C("dve"))
                nmv = b.dma("sync", lambda E, eb=eb, pr=pr, sslot=sslot: E.dma_start(
                    out=stg[:, 1024 + sslot * 128: 1024 + sslot * 128 + 128],
                    in_=w["l1a_wo"][eb * 128:(eb + 1) * 128, pr * 128:(pr + 1) * 128]))
                b.wait("vector", nmv[0], nmv[1])
                b.wait("vector", "pe", b.C("pe"))
                b.emit("vector", lambda E, eb=eb, sslot=sslot: E.tensor_copy(
                    out=stgb[:, eb * 128:(eb + 1) * 128],
                    in_=stg[:, 1024 + sslot * 128: 1024 + sslot * 128 + 128]),
                    selfwait=True)
            cv = b.C("dve")
            b.wait("tensor", "dve", cv)
            b.wait("tensor", "act", b.C("act"))
            for eb in range(NC_):
                b.emit("tensor", lambda E, eb=eb: E.transpose(
                    out=self.p_tp[:, eb * 128:(eb + 1) * 128],
                    in_=stgb[:, eb * 128:(eb + 1) * 128],
                    identity=self.identb[:]), sem="pe" if eb == NC_ - 1 else "")
            pt = b.C("pe")
            b.wait("scalar", "pe", pt)
            b.emit("scalar", lambda E: E.copy(out=woT[:, 0:1024], in_=self.p_tp[:, 0:1024]))
            a_wo = b.C("act")

            for tb in range(4):
                P.job([(lambda E, ps, dc=dc, tb=tb: E.matmul(
                    out=ps[:, 0:512], lhsT=wqT[:, dc * 128:(dc + 1) * 128],
                    rhs=self.xnf[:, dc * T + tb * 512: dc * T + (tb + 1) * 512],
                    start=(dc == 0), stop=(dc == NC_ - 1))) for dc in range(NC_)],
                    lambda E, ps, tb=tb: E.activation(
                        out=qp[:, tb * 512:(tb + 1) * 512], in_=ps[:, 0:512],
                        func=AF.Copy, bias=0.0, scale=0.125),
                    pre=[("act", a_q)])
            for tb in range(4):
                P.job([(lambda E, ps, dc=dc, tb=tb: E.matmul(
                    out=ps[:, 0:512], lhsT=wkT[:, dc * 128:(dc + 1) * 128],
                    rhs=self.xnf[:, dc * T + tb * 512: dc * T + (tb + 1) * 512],
                    start=(dc == 0), stop=(dc == NC_ - 1))) for dc in range(NC_)],
                    lambda E, ps, tb=tb: E.copy(
                        out=kp[:, tb * 512:(tb + 1) * 512], in_=ps[:, 0:512]),
                    pre=[("act", a_k)])
            for tl in range(NT):
                P.job([(lambda E, ps, dc=dc, tl=tl: E.matmul(
                    out=ps[:, 0:128],
                    lhsT=self.xnf[:, dc * T + tl * 128: dc * T + (tl + 1) * 128],
                    rhs=wvT[:, dc * 128:(dc + 1) * 128],
                    start=(dc == 0), stop=(dc == NC_ - 1))) for dc in range(NC_)],
                    lambda E, ps, tl=tl: E.copy(
                        out=vp[:, tl * 128:(tl + 1) * 128], in_=ps[:, 0:128]),
                    pre=[("act", a_v)])
            a_pkv = b.C("act")

            for qt in range(NT):
                for half in range(2):
                    roff = 64 * half
                    klen = (qt + 1) * 128
                    nkb = (klen + 511) // 512
                    fep = [("dve", prev_rm)] if prev_rm else []
                    for kb in range(nkb):
                        wdt = min(512, klen - kb * 512)
                        P.job([lambda E, ps, kb=kb, wdt=wdt, qt=qt, roff=roff: E.matmul(
                            out=ps[:, 0:wdt],
                            lhsT=qp[roff:roff + 64, qt * 128:(qt + 1) * 128],
                            rhs=kp[roff:roff + 64, kb * 512: kb * 512 + wdt],
                            start=True, stop=True)],
                            lambda E, ps, kb=kb, wdt=wdt: E.copy(
                                out=z_bf[:, kb * 512: kb * 512 + wdt],
                                in_=ps[:, 0:wdt]),
                            evac_pre=(fep if kb == 0 else []))
                    a_sc = b.C("act")
                    b.wait("vector", "act", a_sc)
                    b.emit("vector", lambda E, qt=qt: E.tensor_add(
                        out=z_bf[:, qt * 128:(qt + 1) * 128],
                        in0=z_bf[:, qt * 128:(qt + 1) * 128], in1=cmask[:]), selfwait=True)
                    b.emit("vector", lambda E, klen=klen: E.tensor_scalar(
                        out=z_bf[:, 0:klen], in0=z_bf[:, 0:klen], scalar1=1.0,
                        scalar2=None, op0=ALU.mult, op1=ALU.max,
                        accum_out=acol[:, 0:1]), selfwait=True)
                    b.emit("vector", lambda E: E.tensor_scalar(
                        out=acol[:, 1:2], in0=acol[:, 0:1], scalar1=-1.0, scalar2=None,
                        op0=ALU.add), selfwait=True)
                    for it in range(NEWTON_ITERS):
                        b.emit("vector", lambda E, klen=klen: E.scalar_tensor_tensor(
                            out=zj[:, 0:klen], in0=z_bf[:, 0:klen], scalar=acol[:, 1:2],
                            in1=zeros[:, 0:klen], op0=ALU.subtract, op1=ALU.max,
                            accum_out=acol[:, 2:3]), selfwait=True)
                        b.emit("vector", lambda E, klen=klen: E.tensor_scalar(
                            out=zj[:, 0:klen], in0=z_bf[:, 0:klen], scalar1=acol[:, 1:2],
                            scalar2=None, op0=ALU.is_gt, op1=ALU.add,
                            accum_out=acol[:, 3:4]), selfwait=True)
                        b.emit("vector", lambda E: E.reciprocal(
                            out=acol[:, 4:5], in_=acol[:, 3:4]), selfwait=True)
                        b.emit("vector", lambda E: E.tensor_scalar(
                            out=acol[:, 2:3], in0=acol[:, 2:3], scalar1=-1.0, scalar2=None,
                            op0=ALU.add), selfwait=True)
                        b.emit("vector", lambda E: E.tensor_scalar(
                            out=acol[:, 1:2], in0=acol[:, 2:3], scalar1=acol[:, 4:5],
                            scalar2=acol[:, 1:2], op0=ALU.mult, op1=ALU.add),
                            selfwait=True)
                    if prev_tp2 is not None:
                        b.wait("vector", "pe", prev_tp2)
                    b.emit("vector", lambda E, klen=klen: E.tensor_scalar(
                        out=p_bf[:, 0:klen], in0=z_bf[:, 0:klen], scalar1=acol[:, 1:2],
                        scalar2=0.0, op0=ALU.subtract, op1=ALU.max), selfwait=True)
                    v_p = b.C("dve")
                    prev_rm = v_p
                    nkb128 = qt + 1
                    prev_ev8 = None
                    for g0 in range(0, nkb128, 8):
                        gn = min(8, nkb128 - g0)
                        b.wait("tensor", "dve", v_p)
                        if prev_ev8 is not None:
                            b.wait("tensor", "act", prev_ev8)
                        if prev_ctxev is not None and g0 == 0:
                            b.wait("tensor", "act", prev_ctxev)
                        for j in range(gn):
                            kb = g0 + j
                            b.emit("tensor", lambda E, kb=kb, j=j: E.transpose(
                                out=self.p_tp[:, j * 128:(j + 1) * 128],
                                in_=p_bf[:, kb * 128:(kb + 1) * 128],
                                identity=self.identb[:]), sem="pe" if j == gn - 1 else "")
                        pt2 = b.C("pe")
                        b.wait("scalar", "pe", pt2)
                        b.emit("scalar", lambda E, gn=gn: E.copy(
                            out=pT_sb[:, 0:gn * 128], in_=self.p_tp[:, 0:gn * 128]))
                        prev_ev8 = b.C("act")
                        b.wait("tensor", "act", prev_ev8)
                        for j in range(gn):
                            kb = g0 + j
                            b.emit("tensor", lambda E, kb=kb, j=j, qt=qt, roff=roff:
                                   E.matmul(
                                out=p_ctx[0:64, 0:128],
                                lhsT=vp[:, kb * 128 + roff: kb * 128 + roff + 64],
                                rhs=pT_sb[:, j * 128:(j + 1) * 128],
                                start=(kb == 0), stop=(kb == qt)),
                                sem="pe" if kb == qt else "")
                    pt_ctx = b.C("pe")
                    prev_tp2 = pt_ctx
                    b.wait("scalar", "pe", pt_ctx)
                    b.emit("scalar", lambda E, roff=roff, qt=qt: E.copy(
                        out=ctxp[roff:roff + 64, 0:128], in_=p_ctx[0:64, 0:128]))
                    prev_ctxev = b.C("act")
                    if half == 1:
                        for dh in range(2):
                            ps2 = (pC, pD)[dh]
                            key = ("wo", dh)
                            if key in ev:
                                b.wait("tensor", "dve", ev[key])
                            b.wait("tensor", "act", prev_ctxev)
                            b.wait("tensor", "act", a_wo)
                            b.emit("tensor", lambda E, qt=qt, dh=dh, ps2=ps2: E.matmul(
                                out=ps2[:, 0:512],
                                lhsT=ctxp[:, 0:128],
                                rhs=woT[:, dh * 512:(dh + 1) * 512],
                                start=True, stop=True), sem="pe")
                            pmw = b.C("pe")
                            b.wait("vector", "pe", pmw)
                            if pr == 0:
                                b.emit("vector", lambda E, qt=qt, dh=dh, ps2=ps2:
                                       E.tensor_copy(
                                    out=a1acc[:, qt * D + dh * 512: qt * D + (dh + 1) * 512],
                                    in_=ps2[:, 0:512]), selfwait=True)
                            else:
                                b.emit("vector", lambda E, qt=qt, dh=dh, ps2=ps2:
                                       E.tensor_add(
                                    out=a1acc[:, qt * D + dh * 512: qt * D + (dh + 1) * 512],
                                    in0=ps2[:, 0:512],
                                    in1=a1acc[:, qt * D + dh * 512: qt * D + (dh + 1) * 512]),
                                    selfwait=True)
                            ev[key] = b.C("dve")
        v_a1 = b.C("dve")
        b.wait("gpsimd", "dve", v_a1)
        self.a1_sems = []
        for tt in range(NT):
            nmv = b.dma("gpsimd", lambda E, tt=tt: E.dma_start(
                out=self.out[tt * 128:(tt + 1) * 128, :],
                in_=a1acc[:, tt * D:(tt + 1) * D], accum_op=ALU.add))
            self.a1_sems.append(nmv)
        for nm, v in self.a1_sems:
            b.wait("gpsimd", nm, v)
        b.barrier()
        pes.close()
        if self.debug_phase == 5:
            self.tap = dict(a1acc=a1acc[:])
        es5.close()

    def l1_moe_final(self):
        b, nc, w = self.b, self.nc, self.w
        h1 = self.x_acc
        es6 = ExitStack()
        logits = es6.enter_context(nc.sbuf_tensor("logits6", [128, NT * E8], F32))
        wgt = es6.enter_context(nc.sbuf_tensor("wgt6", [128, NT * E8], F32))
        gsc = es6.enter_context(nc.sbuf_tensor("gsc6", [128, 16], F32))
        exprow = es6.enter_context(nc.sbuf_tensor("exprow6", [128, 8], F32))
        ln2c = es6.enter_context(nc.sbuf_tensor("ln2c", [128, 8], F32))
        wr64 = es6.enter_context(nc.sbuf_tensor("wr64", [128, NC_ * 64], BF16))
        stage = es6.enter_context(nc.sbuf_tensor("stage6", [128, 3 * 1024], F32))
        ring = es6.enter_context(nc.sbuf_tensor("ring6", [128, 24 * 1024], BF16))
        a_buf = es6.enter_context(nc.sbuf_tensor("a_buf6", [128, 8 * T], BF16))
        sg_buf = es6.enter_context(nc.sbuf_tensor("sg_buf6", [128, 512], F32))
        fstage = es6.enter_context(nc.sbuf_tensor("fstage6", [128, 2048], F32))
        p_moe = es6.enter_context(nc.psum_tensor("p_moe6", [128, 2048], F32))
        p_sm = es6.enter_context(nc.psum_tensor("p_sm6", [128, 256], F32))
        p_sm2 = es6.enter_context(nc.psum_tensor("p_sm26", [64, 1024], F32))

        n1 = b.dma("sync", lambda E: E.dma_start(
            out=stage[0:8, 0:128],
            in_=w["l1f_ln2"][0:1, :].rearrange("one (c p) -> (one c) p", p=128)))
        b.wait("tensor", n1[0], n1[1])
        b.wait("tensor", "act", b.C("act"))
        b.wait("tensor", "dve", b.C("dve"))
        b.emit("tensor", lambda E: E.transpose(
            out=p_sm[0:128, 8:16], in_=stage[0:8, 0:128],
            identity=self.identf[0:8, 0:8]), sem="pe")
        ptv = b.C("pe")
        b.wait("scalar", "pe", ptv)
        b.wait("sync", "pe", ptv)
        b.emit("scalar", lambda E: E.copy(out=ln2c[:, 0:8], in_=p_sm[0:128, 8:16]))
        a_ln2 = b.C("act")

        rstd2 = self.rstd2
        pe_after = {}
        evac_done = {}
        for tt in range(NT):
            if tt > 0:
                b.wait("vector", "pe", pe_after[tt - 1])
            b.emit("vector", lambda E, tt=tt: E.tensor_scalar(
                out=self.xn_t[:], in0=h1[:, tt * D:(tt + 1) * D],
                scalar1=rstd2[:, tt:tt + 1], scalar2=None, op0=ALU.mult),
                selfwait=(tt == 0))
            v_tile = b.C("dve")
            b.wait("tensor", "dve", v_tile)
            if tt > 0:
                b.wait("tensor", "act", evac_done[tt - 1])
            else:
                b.wait("tensor", "act", b.C("act"))
            for c in range(NC_):
                b.emit("tensor", lambda E, c=c: E.transpose(
                    out=self.p_tp[:, c * 128:(c + 1) * 128],
                    in_=self.xn_t[:, c * 128:(c + 1) * 128],
                    identity=self.identb[:]), sem="pe" if c == NC_ - 1 else "")
            pe_after[tt] = b.C("pe")
            b.wait("scalar", "pe", pe_after[tt])
            for c in range(NC_):
                b.emit("scalar", lambda E, c=c, tt=tt: E.copy(
                    out=self.xnf[:, c * T + tt * 128: c * T + (tt + 1) * 128],
                    in_=self.p_tp[:, c * 128:(c + 1) * 128]))
            evac_done[tt] = b.C("act")
        hnf = self.xnf
        a_hnf = b.C("act")

        b.emit("gpsimd", lambda E: E.memset(wr64[:], 0.0), sem="gp")
        gz = b.C("gp")
        b.wait("vector", "gp", gz)
        b.wait("vector", "act", a_ln2)
        for dc in range(NC_):
            b.wait("sync", "dve", b.C("dve"))
            nmv = b.dma("sync", lambda E, dc=dc: E.dma_start(
                out=stage[:, 1024:1032], in_=w["l1f_wr"][dc * 128:(dc + 1) * 128, :]))
            b.wait("vector", nmv[0], nmv[1])
            b.emit("vector", lambda E, dc=dc: E.tensor_scalar(
                out=wr64[:, dc * 64:dc * 64 + 8], in0=stage[:, 1024:1032],
                scalar1=ln2c[:, dc:dc + 1], scalar2=None, op0=ALU.mult), selfwait=True)
        v_wr = b.C("dve")
        b.wait("tensor", "dve", v_wr)
        b.wait("tensor", "act", a_hnf)
        prev_lf = {}
        for ts4 in range(4):
            par = ts4 % 2
            if par in prev_lf:
                b.wait("tensor", "dve", prev_lf[par])
            for c in range(NC_):
                b.emit("tensor", lambda E, c=c, ts4=ts4, par=par: E.matmul(
                    out=p_sm2[0:64, par * 512:(par + 1) * 512],
                    lhsT=wr64[:, c * 64:(c + 1) * 64],
                    rhs=hnf[:, c * T + ts4 * 512:c * T + (ts4 + 1) * 512],
                    start=(c == 0), stop=(c == NC_ - 1)),
                    sem="pe" if c == NC_ - 1 else "")
            pm = b.C("pe")
            b.wait("vector", "pe", pm)
            b.emit("vector", lambda E, ts4=ts4, par=par: E.tensor_copy(
                out=stage[0:8, ts4 * 512:(ts4 + 1) * 512],
                in_=p_sm2[0:8, par * 512:(par + 1) * 512]), selfwait=True)
            prev_lf[par] = b.C("dve")
        a_lf = b.C("dve")
        b.wait("tensor", "dve", a_lf)
        prev_tt = {}
        for tt in range(NT):
            par = tt % 2
            if par in prev_tt:
                b.wait("tensor", "act", prev_tt[par])
            b.emit("tensor", lambda E, tt=tt, par=par: E.transpose(
                out=p_sm[0:128, par * 8: par * 8 + 8],
                in_=stage[0:8, tt * 128:(tt + 1) * 128],
                identity=self.identf[0:8, 0:8]), sem="pe")
            pm = b.C("pe")
            b.wait("scalar", "pe", pm)
            b.emit("scalar", lambda E, tt=tt, par=par: E.copy(
                out=logits[:, tt * E8:(tt + 1) * E8],
                in_=p_sm[0:128, par * 8: par * 8 + 8]))
            prev_tt[par] = b.C("act")
        a_logits = b.C("act")
        b.wait("vector", "act", a_logits)

        for tt in range(NT):
            lg = wgt[:, tt * E8:(tt + 1) * E8]
            lgs = logits[:, tt * E8:(tt + 1) * E8]
            b.emit("vector", lambda E, lgs=lgs: E.max(out=gsc[:, 0:8], in_=lgs),
                   selfwait=True)
            b.emit("vector", lambda E: E.tensor_scalar(
                out=gsc[:, 8:9], in0=gsc[:, 0:1], scalar1=-1.0, scalar2=None,
                op0=ALU.mult), selfwait=True)
            b.emit("vector", lambda E, lg=lg, lgs=lgs: E.tensor_scalar(
                out=lg, in0=lgs, scalar1=gsc[:, 1:2], scalar2=None,
                op0=ALU.is_ge), selfwait=True)
            v_m = b.C("dve")
            b.wait("scalar", "dve", v_m)
            b.emit("scalar", lambda E, lgs=lgs: E.activation(
                out=exprow[:], in_=lgs, func=AF.Exp, bias=gsc[:, 8:9], scale=1.0),
                selfwait=True)
            a_e = b.C("act")
            b.wait("vector", "act", a_e)
            b.emit("vector", lambda E, lg=lg: E.scalar_tensor_tensor(
                out=lg, in0=lg, scalar=1.0, in1=exprow[:], op0=ALU.mult, op1=ALU.mult,
                accum_out=gsc[:, 8:9]), selfwait=True)
            b.emit("vector", lambda E: E.reciprocal(out=gsc[:, 9:10], in_=gsc[:, 8:9]),
                   selfwait=True)
            b.emit("vector", lambda E, lg=lg: E.tensor_scalar(
                out=lg, in0=lg, scalar1=gsc[:, 9:10], scalar2=None, op0=ALU.mult),
                selfwait=True)
        v_gates = b.C("dve")
        b.wait("sync", "pe", b.C("pe"))

        stage_free = {}
        ring_free = {}
        blk_i = [0]

        def prep6(dram_fn, slot, fold_col=None):
            i = blk_i[0]; blk_i[0] += 1
            ss = i % 3
            if ss in stage_free:
                b.wait("sync", "dve", stage_free[ss])
            nmv = b.dma("sync", lambda E, ss=ss, f=dram_fn: E.dma_start(
                out=stage[:, ss * 1024:(ss + 1) * 1024], in_=f()))
            b.wait("vector", nmv[0], nmv[1])
            if slot in ring_free:
                b.wait("vector", "pe", ring_free[slot])
            if fold_col is not None:
                b.emit("vector", lambda E, ss=ss, slot=slot, fc=fold_col: E.tensor_scalar(
                    out=ring[:, slot * 1024:(slot + 1) * 1024],
                    in0=stage[:, ss * 1024:(ss + 1) * 1024],
                    scalar1=ln2c[:, fc:fc + 1], scalar2=None, op0=ALU.mult))
            else:
                b.emit("vector", lambda E, ss=ss, slot=slot: E.tensor_copy(
                    out=ring[:, slot * 1024:(slot + 1) * 1024],
                    in_=stage[:, ss * 1024:(ss + 1) * 1024]))
            stage_free[ss] = b.C("dve")
            return b.C("dve")

        psum_free = {}
        for e in range(E8):
            for sg in range(2):
                w_ready = None
                for dc in range(NC_):
                    prep6(lambda e=e, dc=dc, sg=sg: w["l1f_wg"][
                        e, dc * 128:(dc + 1) * 128, sg * 1024:(sg + 1) * 1024],
                        dc, fold_col=dc)
                    w_ready = prep6(lambda e=e, dc=dc, sg=sg: w["l1f_wu"][
                        e, dc * 128:(dc + 1) * 128, sg * 1024:(sg + 1) * 1024],
                        8 + dc, fold_col=dc)
                d_ready = None
                for ic in range(NC_):
                    d_ready = prep6(lambda e=e, ic=ic, sg=sg: w["l1f_wd"][
                        e, (sg * 8 + ic) * 128:(sg * 8 + ic + 1) * 128, :],
                        16 + ic, fold_col=None)
                b.wait("tensor", "dve", w_ready)
                for fc in range(NC_):
                    for tb in range(4):
                        qg, qu = tb % 2, 2 + tb % 2
                        if qg in psum_free:
                            b.wait("tensor", "dve", psum_free[qg])
                        for dc in range(NC_):
                            b.emit("tensor", lambda E, dc=dc, tb=tb, qg=qg, fc=fc: E.matmul(
                                out=p_moe[:, qg * 512:(qg + 1) * 512],
                                lhsT=ring[:, dc * 1024 + fc * 128: dc * 1024 + (fc + 1) * 128],
                                rhs=hnf[:, dc * T + tb * 512: dc * T + (tb + 1) * 512],
                                start=(dc == 0), stop=(dc == NC_ - 1)), sem="")
                        if qu in psum_free:
                            b.wait("tensor", "dve", psum_free[qu])
                        for dc in range(NC_):
                            b.emit("tensor", lambda E, dc=dc, tb=tb, qu=qu, fc=fc: E.matmul(
                                out=p_moe[:, qu * 512:(qu + 1) * 512],
                                lhsT=ring[:, (8 + dc) * 1024 + fc * 128:
                                          (8 + dc) * 1024 + (fc + 1) * 128],
                                rhs=hnf[:, dc * T + tb * 512: dc * T + (tb + 1) * 512],
                                start=(dc == 0), stop=(dc == NC_ - 1)),
                                sem="pe" if dc == NC_ - 1 else "")
                        pmu = b.C("pe")
                        b.wait("scalar", "pe", pmu)
                        b.wait("scalar", "dve", b.C("dve"))
                        b.emit("scalar", lambda E, qg=qg: E.activation(
                            out=sg_buf[:], in_=p_moe[:, qg * 512:(qg + 1) * 512],
                            func=AF.Silu, bias=0.0, scale=1.0), selfwait=True)
                        a_sg = b.C("act")
                        b.wait("vector", "act", a_sg)
                        b.emit("vector", lambda E, qu=qu, fc=fc, tb=tb: E.tensor_mul(
                            out=a_buf[:, fc * T + tb * 512: fc * T + (tb + 1) * 512],
                            in0=p_moe[:, qu * 512:(qu + 1) * 512], in1=sg_buf[:]),
                            selfwait=True)
                        vv = b.C("dve")
                        psum_free[qg] = vv
                        psum_free[qu] = vv
                v_a = b.C("dve")
                b.wait("tensor", "dve", v_a)
                for tt in range(NT):
                    par = tt % 2
                    k0, k1 = 2 * par, 2 * par + 1
                    if k0 in psum_free:
                        b.wait("tensor", "dve", psum_free[k0])
                    if k1 in psum_free:
                        b.wait("tensor", "dve", psum_free[k1])
                    for ic in range(NC_):
                        for dh in range(2):
                            b.emit("tensor", lambda E, ic=ic, dh=dh, tt=tt, par=par:
                                   E.matmul(
                                out=p_moe[:, par * 1024 + dh * 512:
                                          par * 1024 + (dh + 1) * 512],
                                lhsT=a_buf[:, ic * T + tt * 128: ic * T + (tt + 1) * 128],
                                rhs=ring[:, (16 + ic) * 1024 + dh * 512:
                                         (16 + ic) * 1024 + (dh + 1) * 512],
                                start=(ic == 0), stop=(ic == NC_ - 1)),
                                sem="pe" if (ic == NC_ - 1 and dh == 1) else "")
                    pmo = b.C("pe")
                    b.wait("vector", "pe", pmo)
                    b.emit("vector", lambda E, tt=tt, par=par, e=e: E.scalar_tensor_tensor(
                        out=h1[:, tt * D:(tt + 1) * D],
                        in0=p_moe[:, par * 1024:(par + 1) * 1024],
                        scalar=wgt[:, tt * E8 + e: tt * E8 + e + 1],
                        in1=h1[:, tt * D:(tt + 1) * D],
                        op0=ALU.mult, op1=ALU.add), selfwait=True)
                    ve = b.C("dve")
                    psum_free[k0] = ve
                    psum_free[k1] = ve
                for s in range(24):
                    ring_free[s] = b.C("pe")

        vfin = b.C("dve")
        b.wait("vector", "dve", vfin)
        self.fin_sems = []
        prev_f = {}
        for tt in range(NT):
            par = tt % 2
            if par in prev_f:
                nm, v = prev_f[par]
                b.wait("vector", nm, v)
            b.emit("vector", lambda E, tt=tt, par=par: E.tensor_scalar(
                out=fstage[:, par * 1024:(par + 1) * 1024],
                in0=h1[:, tt * D:(tt + 1) * D], scalar1=100.0, scalar2=-100.0,
                op0=ALU.min, op1=ALU.max), selfwait=True)
            vc = b.C("dve")
            b.wait("gpsimd", "dve", vc)
            nmv = b.dma("gpsimd", lambda E, tt=tt, par=par: E.dma_start(
                out=self.out[tt * 128:(tt + 1) * 128, :],
                in_=fstage[:, par * 1024:(par + 1) * 1024],
                accum_op=ALU.add))
            prev_f[par] = nmv
            self.fin_sems.append(nmv)
        for nm, v in self.fin_sems:
            b.wait("gpsimd", nm, v)
        b.barrier()
        es6.close()

    def finish_debug(self):
        b = self.b
        for tt in range(NT):
            b.emit("gpsimd", lambda E, tt=tt: E.dma_start(
                out=self.dbg["dbg_a"][:, tt * D:(tt + 1) * D],
                in_=self.x_acc[:, tt * D:(tt + 1) * D]), sem="odma", k=16)
        b.emit("gpsimd", lambda E: E.dma_start(
            out=self.dbg["dbg_b"][:, 0:64], in_=self.stats[:]), sem="odma", k=16)
        for tt in range(NT):
            b.emit("gpsimd", lambda E, tt=tt: E.dma_start(
                out=self.out[tt * 128:(tt + 1) * 128, :],
                in_=self.x_acc[:, tt * D:(tt + 1) * D]), sem="odma", k=16)
        fin = b.C("odma")
        b.wait("gpsimd", "odma", fin)

    def finish_stub(self):
        b = self.b
        b.barrier()
        for tt in range(NT):
            b.emit("gpsimd", lambda E, tt=tt: E.dma_start(
                out=self.out[tt * 128:(tt + 1) * 128, :],
                in_=self.x_acc[:, tt * D:(tt + 1) * D]), sem="odma", k=16)
        fin = b.C("odma")
        b.wait("gpsimd", "odma", fin)

    def build(self):
        self.io()
        self.phase0()
        if self.debug_phase == 1:
            self.finish_debug()
        else:
            self.mla_prep()
            self.mla()
            if self.debug_phase == 2:
                self.finish_debug()
                self.b.replay(self.es.enter_context(self.nc.Block()))
                self.es.close()
                return self.nc
            self.moe0()
            if self.debug_phase == 3:
                self.finish_debug()
            else:
                self.l1_ln_spill()
                self.l1_ffn_attn()
                if self.debug_phase == 4:
                    self.finish_debug()
                else:
                    self.l1_sparsemax()
                    if self.debug_phase == 5:
                        self.finish_debug()
                        self.b.replay(self.es.enter_context(self.nc.Block()))
                        self.es.close()
                        return self.nc
                    self.l1_moe_final()
                    fin = self.b.C("odma")
                    self.b.barrier()
        self.b.replay(self.es.enter_context(self.nc.Block()))
        self.es.close()
        return self.nc


def build_program(debug_phase=0):
    mk = MK(debug_phase)
    nc = mk.build()
    nc._mk = mk
    return nc


def make_rope_tables():
    inv = 1.0 / 10000.0 ** (np.arange(0, HD, 2, dtype=np.float32) / HD)
    fr = np.arange(T, dtype=np.float32)[:, None] * inv
    return np.cos(fr).astype(np.float32), np.sin(fr).astype(np.float32)


def host_inputs(inputs):
    cos, sin = make_rope_tables()
    base = {
        "l0a_wdq": inputs["L0_attn_W_DQ"], "l0a_wuq": inputs["L0_attn_W_UQ"],
        "l0a_wdkv": inputs["L0_attn_W_DKV"], "l0a_wukv": inputs["L0_attn_W_UKV"],
        "l0a_wqr": inputs["L0_attn_W_QR"], "l0a_wkr": inputs["L0_attn_W_KR"],
        "l0a_wo": inputs["L0_attn_W_O"],
        "l0a_qnw": np.asarray(inputs["L0_attn_q_norm_w"]).reshape(1, HD),
        "l0a_knw": np.asarray(inputs["L0_attn_k_norm_w"]).reshape(1, HD),
        "l0f_rw": inputs["L0_ffn_router_weight"],
        "l0f_rb": np.asarray(inputs["L0_ffn_router_bias"]).reshape(1, E8),
        "l0f_gup": inputs["L0_ffn_gate_up_proj"], "l0f_gupb": inputs["L0_ffn_gate_up_proj_bias"],
        "l0f_dp": inputs["L0_ffn_down_proj"], "l0f_dpb": inputs["L0_ffn_down_proj_bias"],
        "l1a_wq": inputs["L1_attn_Wq"], "l1a_wk": inputs["L1_attn_Wk"],
        "l1a_wv": inputs["L1_attn_Wv"], "l1a_wo": inputs["L1_attn_Wo"],
        "l1f_ln1": np.asarray(inputs["L1_ffn_ln1_w"]).reshape(1, D),
        "l1f_ln2": np.asarray(inputs["L1_ffn_ln2_w"]).reshape(1, D),
        "l1f_wq": inputs["L1_ffn_Wq"], "l1f_wk": inputs["L1_ffn_Wk"],
        "l1f_wv": inputs["L1_ffn_Wv"], "l1f_wo": inputs["L1_ffn_Wo"],
        "l1f_wr": inputs["L1_ffn_Wr"],
        "l1f_wg": inputs["L1_ffn_Wg"], "l1f_wu": inputs["L1_ffn_Wu"],
        "l1f_wd": inputs["L1_ffn_Wd"],
        "rope_cos": cos, "rope_sin": sin,
    }
    return {k: np.ascontiguousarray(np.asarray(v, np.float32)) for k, v in base.items()}


def kernel(**inputs):
    x = np.ascontiguousarray(np.asarray(inputs["L0_attn_x"], np.float32))
    B = x.shape[0]
    base = host_inputs(inputs)
    nc = build_program(DEBUG_PHASE)
    in_maps = []
    for bidx in range(B):
        m = dict(base)
        m["x"] = x[bidx]
        in_maps.append(m)
    res = run_bass_kernel_spmd(nc, in_maps, core_ids=list(range(B)))
    outarr = np.stack([np.asarray(r["out"], np.float32) for r in res.results])
    if DEBUG_PHASE:
        kernel.debug = res.results
    return outarr
